# version 03
# baseline (speedup 1.0000x reference)
import numpy as np
import ml_dtypes
from contextlib import ExitStack

import concourse.bass as bass
import concourse.mybir as mybir
from concourse.bass_utils import run_bass_kernel_spmd

F32 = mybir.dt.float32
BF16 = mybir.dt.bfloat16
AF = mybir.ActivationFunctionType
ALU = mybir.AluOpType
AX = mybir.AxisListType

NT = 16
TPC = 2048
DM = 1024
NH = 16
NE = 8
DF = 2048
MASKV = -3.0e9
MFIX = 24.0
YW = 2053

STAGE = 3
ATT_GATHER = True
ATT_LN2 = True
ATT_EXTRAS = True
ATT_KVQ = True
NBAND = 16
BAND_OPROJ = True
BAND_S0 = True
BAND_O0 = True
S0LVL = 3
PHASE2 = True
NEXP = 8


def _pos_bias(d):
    d = np.asarray(d, dtype=np.float64)
    return (np.exp(-0.5 * (d / 2.0) ** 2)
            + np.cos(2.0 * np.pi * d / 401.0)).astype(np.float32)


def _host_consts(rank):
    a = rank * TPC
    pcol = np.arange(128)
    c = np.arange(132)[None, :]
    dist = pcol[:, None] - c + 2
    base = np.where(np.abs(dist) <= 2, _pos_bias(dist), MASKV).astype(np.float32)
    bb = np.stack([base.copy(), base.copy(), base.copy()])
    if rank == 0:
        bb[0][:, 0:2] = MASKV
        bb[0][:, 2] = _pos_bias(pcol)
        bb[0][0, :] = MASKV
    if rank == 3:
        bb[2][:, 130:132] = MASKV
    k0b = np.empty((128, NT), dtype=np.float32)
    for j in range(NT):
        k0b[:, j] = _pos_bias(a + 128 * j + pcol)
    if rank == 0:
        k0b[:, 0] = MASKV
    r0b = np.full((128, 17), MASKV, dtype=np.float32)
    vm = np.zeros((128, 17), dtype=np.float32)
    for j in range(17):
        kl = 128 * j + pcol - 2
        valid = (kl >= 0) & (kl < TPC)
        r0b[valid, j] = _pos_bias(a + kl[valid])
        vm[valid, j] = 1.0
    sel = np.zeros((16, 5), dtype=np.float32)
    if rank > 0:
        sel[4 * (rank - 1) + 2, 0] = 1.0
        sel[4 * (rank - 1) + 3, 1] = 1.0
    if rank < 3:
        sel[4 * (rank + 1) + 0, 2] = 1.0
        sel[4 * (rank + 1) + 1, 3] = 1.0
    sel[0, 4] = 1.0
    q0dm = np.ones((128, 1), dtype=np.float32)
    if rank == 0:
        q0dm[0, 0] = 0.0
    ownm = np.full((1, 1), 1.0 if rank == 0 else 0.0, dtype=np.float32)
    return dict(bb=bb.reshape(3 * 128, 132), k0b=k0b, r0b=r0b, vm=vm,
                selm=sel.astype(ml_dtypes.bfloat16), q0dm=q0dm, ownm=ownm)


class _Pend:
    __slots__ = ("op", "args", "kwargs", "incs")


class EngProxy:

    def __init__(self, name):
        self._name = name
        self._q = []

    def __getattr__(self, op):
        def call(*a, **k):
            h = _Pend()
            h.op, h.args, h.kwargs, h.incs = op, a, k, []
            self._q.append(h)
            return h
        return call

    def flush(self, e):
        for h in self._q:
            ins = getattr(e, h.op)(*h.args, **h.kwargs)
            for s, k in h.incs:
                ins.then_inc(s, k)


class Sem:
    __slots__ = ("h", "n")

    def __init__(self, h):
        self.h = h
        self.n = 0

    def inc(self, ins, k=1):
        ins.incs.append((self.h, k))
        self.n += k
        return self.n

    def wait(self, eng, val):
        assert val <= self.n, "wait on future semaphore value"
        if val > 0:
            eng.wait_ge(self.h, val)


def build_nc():
    nc = bass.Bass("TRN2", target_bir_lowering=False)

    def din(name, shape, dt=F32):
        return nc.dram_tensor(name, shape, dt, kind="ExternalInput").ap()

    xin = din("xin", [TPC, DM])
    wT_d = {w: din(w, [DM, DM]) for w in ("wqT", "wkT", "wvT", "woT")}
    b_d = {b: din(b, [1, DM]) for b in ("bq", "bk", "bv", "bo")}
    router_d = din("router", [DM, NE])
    wg_d = din("wg", [NE, DM, DF])
    wu_d = din("wu", [NE, DM, DF])
    wd_d = din("wd", [NE, DF, DM])
    bb_d = din("bb", [3 * 128, 132])
    k0b_d = din("k0b", [128, NT])
    r0b_d = din("r0b", [128, 17])
    vm_d = din("vm", [128, 17])
    selm_d = din("selm", [16, 5], BF16)
    q0dm_d = din("q0dm", [128, 1])
    ownm_d = din("ownm", [1, 1])
    identb_d = din("identb", [128, 128], BF16)
    identf_d = din("identf", [128, 128], F32)

    xout = nc.dram_tensor("xout", [TPC, DM], F32, kind="ExternalOutput").ap()

    wgbf = nc.dram_tensor("wgbf", [NE, DM, DF], BF16)
    wubf = nc.dram_tensor("wubf", [NE, DM, DF], BF16)
    cc1i = nc.dram_tensor("cc1i", [4, DM], F32)
    cc1o = nc.dram_tensor("cc1o", [16, DM], F32)
    cc2i = nc.dram_tensor("cc2i", [17, 1025], F32)
    cc2o = nc.dram_tensor("cc2o", [17, 1025], F32)

    with ExitStack() as ctx:
        def sbt(name, shape, dt):
            return ctx.enter_context(nc.sbuf_tensor(name, shape, dt))

        x_sb = sbt("x_sb", [128, NT * DM], F32)
        arenaA = sbt("arenaA", [128, 33856], BF16)
        y2q = sbt("y2q", [128, 8 * YW], BF16)
        arenaB = sbt("arenaB", [128, 16384], BF16)
        stats = sbt("stats", [128, 80], F32)
        fw_sb = sbt("fw_sb", [128, NT * NE], F32)
        lg_sb = sbt("lg_sb", [128, NT * NE], F32)
        m2_sb = sbt("m2_sb", [128, NT * NE], BF16)
        rt_sb = sbt("rt_sb", [128, 48], F32)
        rt_w = sbt("rt_w", [128, 8 * NE], F32)
        bb_sb = sbt("bb_sb", [128, 3 * 132], F32)
        k0b_sb = sbt("k0b_sb", [128, NT], F32)
        r0b_sb = sbt("r0b_sb", [128, 17], F32)
        vm_sb = sbt("vm_sb", [128, 17], F32)
        q0dm_sb = sbt("q0dm_sb", [128, 1], F32)
        ownm_sb = sbt("ownm_sb", [1, 1], F32)
        identb = sbt("identb_s", [128, 128], BF16)
        identf = sbt("identf_s", [128, 128], F32)
        pnT_sb = sbt("pnT_sb", [128, 384], BF16)
        ctxfm = sbt("ctxfm", [128, 1024], BF16)
        dead_sb = sbt("dead_sb", [128, NT * NH], BF16)
        p0e = sbt("p0e", [128, 17], BF16)
        p0t = sbt("p0t", [128, 32], F32)
        q0_sb = sbt("q0_sb", [128, 8], BF16)
        q0m = sbt("q0m", [128, 16], BF16)
        ctx0 = sbt("ctx0", [128, 8], BF16)
        mvsel = sbt("mvsel", [128, 16], BF16)
        brow1 = sbt("brow1", [65, DM], BF16)
        gthr = None
        selm_sb = sbt("selm_sb", [16, 5], BF16)
        deadr = sbt("deadr", [1, 17], BF16)
        mvrow = None
        l17 = sbt("l17", [17, 4], F32)
        v0row_t = sbt("v0row", [1, 1025], BF16)
        negmf = sbt("negmf", [128, 1], F32)

        v0row = vtile_slot16 = None
        pb = [ctx.enter_context(nc.psum_tensor(f"pb{i}", [128, 512], F32))
              for i in range(8)]
        pb6b = pb[6][:, :].bitcast(BF16)
        _stg = arenaB[:, 12288:16384].bitcast(F32)
        _wv1f = arenaB[:, 0:8192].bitcast(F32)
        sstage = _stg[:, 0:1024]
        ytm = _stg[:, 0:1024]
        yfev = _stg[:, 1536:1792]
        o0_sb = _wv1f[0:17, 0:1025]
        p_sb = _wv1f[:, 1030:1562]
        mf_sb = arenaB[:, 3124:3656]
        pn_sb = arenaB[:, 3656:4188]

        S = {n: Sem(ctx.enter_context(nc.semaphore(n)))
             for n in ["ldx0", "ldx1", "ldx2", "ldx3", "ldc", "ldg", "wld", "wdld0", "wdld1", "wdld2",
                       "wdld3", "awk", "awv", "awq", "awo", "gfet", "wldb", "pe", "act",
                       "dve", "cc1d", "cc2d", "st"] +
                      [f"cvt{i}" for i in range(8)]}

        block = ctx.enter_context(nc.Block())
        T, V, C, G, Y = (EngProxy("tensor"), EngProxy("vector"),
                         EngProxy("scalar"), EngProxy("gpsimd"),
                         EngProxy("sync"))

        def wgc(kc):
            return arenaA[:, 2048 * kc:2048 * (kc + 1)]

        def wuc(kc):
            return arenaA[:, 16384 + 2048 * kc:16384 + 2048 * (kc + 1)]

        def kfm(ic):
            return arenaA[:, YW * ic:YW * (ic + 1)]

        def vtile(j):
            return arenaA[:, 16424 + 1025 * j:16424 + 1025 * (j + 1)]

        def ych(kc):
            return y2q[:, YW * kc:YW * (kc + 1)]

        def fch(fc):
            return arenaB[:, 512 * fc:512 * (fc + 1)]

        def wdch(slot):
            return arenaB[:, 8192 + 1024 * slot:8192 + 1024 * (slot + 1)]

        def wslot1(kc):
            return arenaB[:, 1024 * kc:1024 * (kc + 1)]

        def wslot2(kc):
            return arenaB[:, 8192 + 1024 * kc:8192 + 1024 * (kc + 1)]

        def xtile(t):
            return x_sb[:, DM * t:DM * (t + 1)]

        ones64 = vtile(16)[64:65, 0:512]
        v0row = v0row_t[0:1, :]
        ones32 = ctxfm[32:33, 0:512]
        deadt = pn_sb[0:16, 0:128]
        ones0q = ctxfm[0:1, 512:1024]
        ones0b = arenaB[0:1, 4224:4736]
        brow2 = arenaB[0:1, 4736:5760]
        gthr = ctxfm[0:16, :]
        wovm = arenaB[0:16, 12288:13312]
        x1ex = ctxfm[0:5, :]

        xr = xin.rearrange("(t p) d -> p t d", p=128)
        xv = x_sb[:, :].rearrange("p (t d) -> p t d", d=DM)
        for i in range(4):
            S[f"ldx{i}"].inc(Y.dma_start(out=xv[:, 4 * i:4 * i + 4, :],
                                        in_=xr[:, 4 * i:4 * i + 4, :]), 16)
        e_ldx = 16
        for dst, src in [(bb_sb[:, :].rearrange("p (v c) -> p v c", v=3),
                          bb_d.rearrange("(v p) c -> p v c", p=128)),
                         (k0b_sb, k0b_d), (r0b_sb, r0b_d), (vm_sb, vm_d),
                         (q0dm_sb, q0dm_d), (ownm_sb, ownm_d),
                         (identb, identb_d), (identf, identf_d),
                         (selm_sb, selm_d)]:
            S["ldc"].inc(Y.dma_start(
                out=dst if isinstance(dst, bass.AP) else dst[:, :],
                in_=src), 16)
        for kc in range(8):
            S["ldc"].inc(Y.dma_start(out=rt_w[:, 8 * kc:8 * kc + 8],
                                     in_=router_d[128 * kc:128 * kc + 128, :]), 16)
        S["ldg"].inc(G.dma_start(out=brow1[0:1, :], in_=b_d["bq"]), 16)
        S["ldg"].inc(G.dma_start(out=brow1[32:33, :], in_=b_d["bk"]), 16)
        S["ldg"].inc(G.dma_start(out=brow1[64:65, :], in_=b_d["bv"]), 16)
        e_ldc = S["ldc"].n
        e_ldg = S["ldg"].n
        V.memset(ones64[:, :], 1.0)
        V.memset(negmf[:, :], -MFIX)

        wd_state = {"seq": 0, "evdma": {}, "evfree": {}}
        conv_order = [2, 3, 4, 5, 6, 7, 0, 1]
        cvt_todo = [(e, w, kc) for e in conv_order for w in (0, 1)
                    for kc in range(8)]

        def emit_ln_sweep(e_x_dve, first=False):
            S["dve"].wait(V, e_x_dve)
            ev = 0
            for t in range(NT):
                if first and t % 4 == 0:
                    S[f"ldx{t // 4}"].wait(V, 16)
                V.bn_stats(stats[:, 64:70], xtile(t)[:, 0:512])
                e = S["dve"].inc(V.bn_stats(stats[:, 70:76], xtile(t)[:, 512:1024]))
                S["dve"].wait(V, e)
                e = S["dve"].inc(V.bn_aggr(stats[:, 76:78], stats[:, 64:76]))
                S["dve"].wait(V, e)
                V.tensor_copy(stats[:, t:t + 1], stats[:, 76:77])
                ev = S["dve"].inc(V.tensor_scalar_add(stats[:, 32 + t:33 + t],
                                                      stats[:, 77:78], 1e-5))
            S["dve"].wait(C, ev)
            e_sig = S["act"].inc(C.activation(stats[:, 48:64], stats[:, 32:48],
                                              AF.Sqrt))
            S["act"].wait(V, e_sig)
            return S["dve"].inc(V.reciprocal(stats[:, 16:32], stats[:, 48:64]))

        def emit_ffn(first_call, e_x_dve, e_arenaA_pe, e_arenaB_pe):
            e_rstd = emit_ln_sweep(e_x_dve, first=first_call)

            ev_lg = {}
            prev_cast = [0, 0]
            prev_mm = [0, 0]
            S["pe"].wait(V, e_arenaB_pe)
            for t in range(NT):
                S["dve"].wait(V, e_rstd)
                e_ytm = S["dve"].inc(V.tensor_scalar(
                    ytm[:, :], xtile(t), stats[:, t:t + 1], stats[:, 16 + t:17 + t],
                    ALU.subtract, ALU.mult))
                S["dve"].wait(T, e_ytm)
                if t == 0:
                    S["ldc"].wait(T, e_ldc)
                for kc in range(8):
                    buf = kc % 2
                    S["dve"].wait(T, prev_cast[buf])
                    S["pe"].wait(T, prev_mm[buf])
                    e_tr = S["pe"].inc(T.transpose(pb[4 + buf][:, 0:128],
                                                   ytm[:, 128 * kc:128 * (kc + 1)],
                                                   identf[:, :]))
                    S["pe"].wait(C, e_tr)
                    e_ev = S["act"].inc(C.copy(yfev[:, 128 * buf:128 * (buf + 1)],
                                               pb[4 + buf][:, 0:128]))
                    S["act"].wait(V, e_ev)
                    prev_cast[buf] = S["dve"].inc(V.tensor_copy(
                        ych(kc)[:, 128 * t:128 * (t + 1)],
                        yfev[:, 128 * buf:128 * (buf + 1)]))
                    S["act"].wait(T, e_ev)
                    prev_mm[buf] = S["pe"].inc(T.matmul(
                        pb[6][:, 0:8], yfev[:, 128 * buf:128 * (buf + 1)],
                        rt_w[:, 8 * kc:8 * kc + 8],
                        start=(kc == 0), stop=(kc == 7)))
                S["pe"].wait(V, prev_mm[1])
                ev_lg[t] = S["dve"].inc(V.tensor_copy(lg_sb[:, 8 * t:8 * t + 8],
                                                      pb[6][:, 0:8]))
            e_ycast = S["dve"].n

            e_stash = 0
            for t in range(NT):
                L = lg_sb[:, 8 * t:8 * t + 8]
                S["dve"].wait(V, ev_lg[t])
                e = S["dve"].inc(V.reduce_max(rt_sb[:, 0:1], L, axis=AX.X))
                S["dve"].wait(V, e)
                e = S["dve"].inc(V.tensor_scalar(rt_sb[:, 2:10], L, rt_sb[:, 0:1],
                                                 None, ALU.is_ge))
                S["dve"].wait(V, e)
                e = S["dve"].inc(V.scalar_tensor_tensor(
                    rt_sb[:, 10:18], rt_sb[:, 2:10], -1.0e30, L, ALU.mult, ALU.add))
                S["dve"].wait(V, e)
                e = S["dve"].inc(V.reduce_max(rt_sb[:, 1:2], rt_sb[:, 10:18],
                                              axis=AX.X))
                S["dve"].wait(V, e)
                e = S["dve"].inc(V.tensor_scalar(rt_sb[:, 18:26], rt_sb[:, 10:18],
                                                 rt_sb[:, 1:2], None, ALU.is_ge))
                ng = S["dve"].inc(V.tensor_scalar_mul(rt_sb[:, 40:41],
                                                      rt_sb[:, 0:1], -1.0))
                S["dve"].wait(C, ng)
                e_exp = S["act"].inc(C.activation(rt_sb[:, 26:34], L, AF.Exp,
                                                  bias=rt_sb[:, 40:41], scale=1.0,
                                                  accum_out=rt_sb[:, 34:35]))
                S["act"].wait(V, e_exp)
                e = S["dve"].inc(V.tensor_mul(rt_sb[:, 10:18], rt_sb[:, 26:34],
                                              rt_sb[:, 2:10]))
                S["dve"].wait(V, e)
                e = S["dve"].inc(V.reduce_sum(rt_sb[:, 35:36], rt_sb[:, 10:18],
                                              axis=AX.X))
                S["dve"].wait(V, e)
                e = S["dve"].inc(V.tensor_mul(rt_sb[:, 10:18], rt_sb[:, 26:34],
                                              rt_sb[:, 18:26]))
                S["dve"].wait(V, e)
                e = S["dve"].inc(V.reduce_sum(rt_sb[:, 36:37], rt_sb[:, 10:18],
                                              axis=AX.X))
                S["dve"].wait(V, e)
                S["dve"].inc(V.tensor_sub(rt_sb[:, 37:38], rt_sb[:, 35:36],
                                          rt_sb[:, 36:37]))
                e = S["dve"].inc(V.reciprocal(rt_sb[:, 38:39], rt_sb[:, 34:35]))
                S["dve"].wait(V, e)
                e = S["dve"].inc(V.tensor_mul(rt_sb[:, 39:40], rt_sb[:, 37:38],
                                              rt_sb[:, 38:39]))
                S["dve"].wait(V, e)
                S["dve"].inc(V.tensor_copy(fw_sb[:, 8 * t:8 * t + 1],
                                           rt_sb[:, 39:40]))
                S["dve"].inc(V.tensor_copy(L, rt_sb[:, 2:10]))
                e_stash = S["dve"].inc(V.tensor_copy(m2_sb[:, 8 * t:8 * t + 8],
                                                     rt_sb[:, 18:26]))
            argall = fw_sb[:, :].rearrange("p (t e) -> p t e", e=NE)[:, :, 0]
            S["dve"].wait(C, e_stash)
            e_sg = S["act"].inc(C.activation(stats[:, 48:64], argall, AF.Sigmoid))
            S["act"].wait(V, e_sg)
            e_fw = 0
            for t in range(NT):
                nw1 = stats[:, 48 + t:49 + t]
                S["dve"].wait(V, e_fw if t > 0 else 0)
                S["dve"].inc(V.tensor_scalar(rt_sb[:, 2:10],
                                             lg_sb[:, 8 * t:8 * t + 8],
                                             nw1, 0.5, ALU.mult, ALU.mult))
                e = S["dve"].inc(V.tensor_scalar(rt_sb[:, 10:18],
                                                 m2_sb[:, 8 * t:8 * t + 8],
                                                 nw1, 0.5, ALU.mult, ALU.mult))
                S["dve"].wait(V, e)
                e = S["dve"].inc(V.scalar_tensor_tensor(
                    rt_sb[:, 10:18], m2_sb[:, 8 * t:8 * t + 8], 0.5,
                    rt_sb[:, 10:18], ALU.mult, ALU.subtract))
                S["dve"].wait(V, e)
                e_fw = S["dve"].inc(V.tensor_add(fw_sb[:, 8 * t:8 * t + 8],
                                                 rt_sb[:, 2:10], rt_sb[:, 10:18]))

            def load_wgu(e_idx, wait_pe):
                staged = (not first_call) or e_idx >= 2
                if not staged:
                    S["pe"].wait(G, wait_pe)
                    for kc in range(8):
                        S["wld"].inc(G.dma_start(
                            out=wgc(kc),
                            in_=wg_d[e_idx, 128 * kc:128 * (kc + 1), :]), 16)
                    for kc in range(8):
                        S["wld"].inc(G.dma_start(
                            out=wuc(kc),
                            in_=wu_d[e_idx, 128 * kc:128 * (kc + 1), :]), 16)
                    return ("wld", S["wld"].n)
                S["pe"].wait(Y, wait_pe)
                assert S[f"cvt{e_idx}"].n == 256, "conversion not fully emitted"
                S[f"cvt{e_idx}"].wait(Y, 256)
                for kc in range(8):
                    S["wldb"].inc(Y.dma_start(
                        out=wgc(kc), in_=wgbf[e_idx, 128 * kc:128 * (kc + 1), :]), 16)
                for kc in range(8):
                    S["wldb"].inc(Y.dma_start(
                        out=wuc(kc), in_=wubf[e_idx, 128 * kc:128 * (kc + 1), :]), 16)
                return ("wldb", S["wldb"].n)

            wd_plan = [(e_i, fc) for e_i in range(NEXP) for _s in range(4)
                       for fc in range(16)]
            base_seq = wd_state["seq"]

            def load_wd_next(i_local, wait_pe_extra=0):
                seq = wd_state["seq"]
                e_i, fc = wd_plan[i_local]
                if wait_pe_extra:
                    S["pe"].wait(G, wait_pe_extra)
                if seq >= 4:
                    S["pe"].wait(G, wd_state["evfree"][seq - 4])
                wd_state["evdma"][seq] = S[f"wdld{seq % 4}"].inc(G.dma_start(
                    out=wdch(seq % 4),
                    in_=wd_d[e_i, 128 * fc:128 * (fc + 1), :]), 16)
                wd_state["seq"] += 1
                if first_call and cvt_todo:
                    ce, cw, ckc = cvt_todo.pop(0)
                    dsrc = wg_d if cw == 0 else wu_d
                    ddst = wgbf if cw == 0 else wubf
                    S[f"cvt{ce}"].inc(G.dma_start(
                        out=ddst[ce, 128 * ckc:128 * (ckc + 1), :],
                        in_=dsrc[ce, 128 * ckc:128 * (ckc + 1), :]), 16)

            if NEXP > 0:
                ev_wgu = load_wgu(0, e_arenaA_pe)
                for i in range(min(4, len(wd_plan))):
                    load_wd_next(i, e_arenaB_pe if i == 0 else 0)
            n_wd_loaded = min(4, len(wd_plan))

            silu_free = [0, 0]
            fmul_free = [0, 0]
            ev_fmul_fc = {}
            ev_xupd = [0] * 8
            e_p1_last = 0
            e_phase2_read = e_arenaB_pe
            for ex in range(NEXP):
                S[ev_wgu[0]].wait(T, ev_wgu[1])
                if ex == 0:
                    S["dve"].wait(T, e_fw)
                    S["dve"].wait(T, e_ycast)
                for s in range(4):
                    for fc in range(16):
                        gb, ub = pb[fc % 2], pb[2 + fc % 2]
                        S["act"].wait(T, silu_free[fc % 2])
                        S["dve"].wait(T, fmul_free[fc % 2])
                        if fc < 2:
                            S["dve"].wait(T, ev_xupd[fc % 2])
                            S["dve"].wait(T, ev_xupd[2 + fc % 2])
                        if fc == 0:
                            S["pe"].wait(V, e_phase2_read)
                        for kc in range(8):
                            ins = T.matmul(gb[:, :],
                                           wgc(kc)[:, 128 * fc:128 * (fc + 1)],
                                           ych(kc)[:, 512 * s:512 * (s + 1)],
                                           start=(kc == 0), stop=(kc == 7))
                        eg = S["pe"].inc(ins)
                        for kc in range(8):
                            ins = T.matmul(ub[:, :],
                                           wuc(kc)[:, 128 * fc:128 * (fc + 1)],
                                           ych(kc)[:, 512 * s:512 * (s + 1)],
                                           start=(kc == 0), stop=(kc == 7))
                        eu = S["pe"].inc(ins)
                        e_p1_last = eu
                        S["pe"].wait(C, eg)
                        silu_free[fc % 2] = S["act"].inc(C.activation(
                            sstage[:, 512 * (fc % 2):512 * (fc % 2 + 1)],
                            gb[:, :], AF.Silu))
                        S["pe"].wait(V, eu)
                        S["act"].wait(V, silu_free[fc % 2])
                        fmul_free[fc % 2] = S["dve"].inc(V.tensor_mul(
                            fch(fc),
                            sstage[:, 512 * (fc % 2):512 * (fc % 2 + 1)],
                            ub[:, :]))
                        ev_fmul_fc[fc] = fmul_free[fc % 2]
                    if not PHASE2:
                        continue
                    for k in range(8):
                        S["dve"].wait(T, ev_xupd[k])
                    S["act"].wait(T, silu_free[0])
                    S["act"].wait(T, silu_free[1])
                    S["dve"].wait(T, fmul_free[0])
                    S["dve"].wait(T, fmul_free[1])
                    for fc in range(16):
                        seq = base_seq + (ex * 4 + s) * 16 + fc
                        S[f"wdld{seq % 4}"].wait(T, wd_state["evdma"][seq])
                        S["dve"].wait(T, ev_fmul_fc[fc])
                        last = None
                        for tt in range(4):
                            for dh in range(2):
                                last = T.matmul(
                                    pb[tt * 2 + dh][:, :],
                                    fch(fc)[:, 128 * tt:128 * (tt + 1)],
                                    wdch(seq % 4)[:, 512 * dh:512 * (dh + 1)],
                                    start=(fc == 0), stop=(fc == 15))
                        wd_state["evfree"][seq] = S["pe"].inc(last)
                        if n_wd_loaded < len(wd_plan):
                            load_wd_next(n_wd_loaded)
                            n_wd_loaded += 1
                    e_p2 = S["pe"].n
                    if s == 3 and ex + 1 < NEXP:
                        ev_wgu = load_wgu(ex + 1, e_p1_last)
                    S["dve"].wait(V, e_fw)
                    for tt in range(4):
                        t_glob = 4 * s + tt
                        for dh in range(2):
                            S["pe"].wait(V, e_p2)
                            ev_xupd[tt * 2 + dh] = S["dve"].inc(
                                V.scalar_tensor_tensor(
                                    xtile(t_glob)[:, 512 * dh:512 * (dh + 1)],
                                    pb[tt * 2 + dh][:, :],
                                    fw_sb[:, 8 * t_glob + ex:8 * t_glob + ex + 1],
                                    xtile(t_glob)[:, 512 * dh:512 * (dh + 1)],
                                    ALU.mult, ALU.add))
                    e_phase2_read = e_p2
            return S["dve"].n, S["pe"].n

        e_ffn1, e_ffn1_pe = emit_ffn(True, 0, 0, 0)
        e_final_dve = e_ffn1

        if STAGE >= 2:
            e_gthr = 0
            if ATT_GATHER:
                S["dve"].wait(Y, e_ffn1)
                S["st"].inc(Y.dma_start(out=cc1i[0:2, :], in_=x_sb[0:2, 0:DM]), 16)
                e_cc1i = S["st"].inc(Y.dma_start(
                    out=cc1i[2:4, :], in_=x_sb[126:128, DM * 15:DM * 16]), 16)
                S["st"].wait(G, e_cc1i)
                cci = G.collective_compute(
                    "AllGather", ALU.bypass,
                    replica_groups=[[0, 1, 2, 3], [4, 5, 6, 7]],
                    ins=[cc1i.ap().opt()], outs=[cc1o.ap().opt()])
                S["cc1d"].inc(cci, 1)
                G.wait_ge(S["cc1d"].h, 1)
                e_gthr = S["gfet"].inc(G.dma_start(out=gthr[:, :], in_=cc1o[:, :]), 16)

            e_ln2 = S["dve"].n
            if ATT_LN2:
                e_rstd = emit_ln_sweep(e_ffn1)
                prev_cast = [0, 0]
                for t in range(NT):
                    S["dve"].wait(V, e_rstd)
                    e_ytm = S["dve"].inc(V.tensor_scalar(
                        ytm[:, :], xtile(t), stats[:, t:t + 1], stats[:, 16 + t:17 + t],
                        ALU.subtract, ALU.mult))
                    S["dve"].wait(T, e_ytm)
                    if t == 0:
                        S["pe"].wait(T, e_ffn1_pe)
                    for kc in range(8):
                        buf = kc % 2
                        S["dve"].wait(T, prev_cast[buf])
                        e_tr = S["pe"].inc(T.transpose(pb[4 + buf][:, 0:128],
                                                       ytm[:, 128 * kc:128 * (kc + 1)],
                                                       identf[:, :]))
                        S["pe"].wait(V, e_tr)
                        prev_cast[buf] = S["dve"].inc(V.tensor_copy(
                            ych(kc)[:, 2 + 128 * t:2 + 128 * (t + 1)],
                            pb[4 + buf][:, 0:128]))
                e_ln2 = S["dve"].n

            e_y2full = S["dve"].n
            if ATT_EXTRAS:
                S["gfet"].wait(T, e_gthr)
                e_x1ex = 0
                for half in range(2):
                    if half == 0:
                        S["dve"].wait(T, e_ln2)
                    else:
                        S["dve"].wait(T, e_x1ex)
                    e_mm = S["pe"].inc(T.matmul(pb[6][0:5, :], selm_sb[:, :],
                                                gthr[:, 512 * half:512 * (half + 1)],
                                                start=True, stop=True))
                    S["pe"].wait(V, e_mm)
                    e_x1ex = S["dve"].inc(V.tensor_copy(
                        x1ex[0:5, 512 * half:512 * (half + 1)], pb[6][0:5, :]))
                S["dve"].wait(V, e_x1ex)
                V.bn_stats(stats[0:5, 64:70], x1ex[0:5, 0:512])
                e = S["dve"].inc(V.bn_stats(stats[0:5, 70:76], x1ex[0:5, 512:1024]))
                S["dve"].wait(V, e)
                e = S["dve"].inc(V.bn_aggr(stats[0:5, 76:78], stats[0:5, 64:76]))
                S["dve"].wait(V, e)
                e = S["dve"].inc(V.tensor_scalar_add(stats[0:5, 77:78],
                                                     stats[0:5, 77:78], 1e-5))
                S["dve"].wait(C, e)
                e = S["act"].inc(C.activation(stats[0:5, 78:79], stats[0:5, 77:78],
                                              AF.Sqrt))
                S["act"].wait(V, e)
                e = S["dve"].inc(V.reciprocal(stats[0:5, 79:80], stats[0:5, 78:79]))
                S["dve"].wait(V, e)
                e_yex = S["dve"].inc(V.tensor_scalar(
                    ytm[0:5, :], x1ex[0:5, :], stats[0:5, 76:77], stats[0:5, 79:80],
                    ALU.subtract, ALU.mult))
                S["dve"].wait(T, e_yex)
                e_y2full = 0
                for kc in range(8):
                    if kc > 0:
                        S["dve"].wait(T, e_y2full)
                    e_tr = S["pe"].inc(T.transpose(pb[4][:, 0:5],
                                                   ytm[0:5, 128 * kc:128 * (kc + 1)],
                                                   identf[0:5, 0:5]))
                    S["pe"].wait(V, e_tr)
                    V.tensor_copy(ych(kc)[:, 0:2], pb[4][:, 0:2])
                    V.tensor_copy(ych(kc)[:, 2050:2052], pb[4][:, 2:4])
                    e_y2full = S["dve"].inc(V.tensor_copy(ych(kc)[:, 2052:2053],
                                                          pb[4][:, 4:5]))

            S["pe"].wait(V, S["pe"].n)
            e_ones0 = S["dve"].inc(V.memset(ones0q[0:1, :], 1.0))

            def load_wT(dram, slotfn, wait_pe, sem):
                S["pe"].wait(G, wait_pe)
                ev = 0
                for kc in range(8):
                    ev = S[sem].inc(G.dma_start(
                        out=slotfn(kc), in_=dram[128 * kc:128 * (kc + 1), :]), 16)
                return ev

            ev_wk = load_wT(wT_d["wkT"], wslot1, e_ffn1_pe, "awk")

            e_extr_pe = S["pe"].n
            ev_wo = load_wT(wT_d["woT"], wslot2, e_extr_pe, "awo")

            e_q_act = 0
            e_vones = 0
            e_q_pe = 0
            if ATT_KVQ:
                S["pe"].wait(V, e_ffn1_pe)
                e_ones32 = S["dve"].inc(V.memset(ones32[0:1, :], 1.0))
                S["awk"].wait(T, ev_wk)
                S["ldg"].wait(T, e_ldg)
                S["dve"].wait(T, e_ones32)
                kprev = [0, 0]
                for s in range(5):
                    S["dve"].wait(T, e_y2full if s in (0, 4) else e_ln2)
                    for ic in range(8):
                        bank = pb[ic % 2]
                        S["act"].wait(T, kprev[ic % 2])
                        w = 512 if s < 4 else 5
                        c0 = 512 * s if s < 4 else 2048
                        for kc in range(8):
                            T.matmul(bank[:, 0:w], wslot1(kc)[:, 128 * ic:128 * (ic + 1)],
                                     ych(kc)[:, c0:c0 + w], start=(kc == 0), stop=False)
                        e_mm = S["pe"].inc(T.matmul(
                            bank[:, 0:w], brow1[32:33, 128 * ic:128 * (ic + 1)],
                            ones32[0:1, 0:w], start=False, stop=True))
                        S["pe"].wait(C, e_mm)
                        kprev[ic % 2] = S["act"].inc(C.activation(
                            kfm(ic)[:, c0:c0 + w], bank[:, 0:w], AF.Copy, scale=0.125))
                e_k_pe = S["pe"].n

                ev_wv = load_wT(wT_d["wvT"], wslot1, e_k_pe, "awv")
                S["awv"].wait(T, ev_wv)
                vprev = [0, 0]
                e_vones = 0
                for j in range(18):
                    nrow = 128 if j < 16 else (4 if j == 16 else 1)
                    c0 = 128 * j if j < 16 else (2048 if j == 16 else 2052)
                    tgt = vtile(j) if j < 16 else (vtile(16) if j == 16 else v0row)
                    for dh in range(2):
                        bank = pb[2 + dh]
                        S["act"].wait(T, vprev[dh])
                        for kc in range(8):
                            T.matmul(bank[0:nrow, :], ych(kc)[:, c0:c0 + nrow],
                                     wslot1(kc)[:, 512 * dh:512 * (dh + 1)],
                                     start=(kc == 0), stop=False)
                        e_mm = S["pe"].inc(T.matmul(
                            bank[0:nrow, :], ones64[0:1, 0:nrow],
                            brow1[64:65, 512 * dh:512 * (dh + 1)],
                            start=False, stop=True))
                        S["pe"].wait(C, e_mm)
                        vprev[dh] = S["act"].inc(C.copy(
                            tgt[0:nrow, 512 * dh:512 * (dh + 1)], bank[0:nrow, :]))
                    e_vones = S["dve"].inc(V.memset(tgt[0:nrow, 1024:1025], 1.0))
                e_v_pe = S["pe"].n

                ev_wq = load_wT(wT_d["wqT"], wslot1, e_v_pe, "awq")
                S["awq"].wait(T, ev_wq)
                e_q0 = 0
                S["dve"].wait(T, e_ones0)
                for ic in range(8):
                    bank = pb[ic % 2]
                    S["act"].wait(T, vprev[ic % 2] if ic < 2 else e_q0)
                    for kc in range(8):
                        T.matmul(bank[:, 0:1], wslot1(kc)[:, 128 * ic:128 * (ic + 1)],
                                 ych(kc)[:, 2052:2053], start=(kc == 0), stop=False)
                    e_mm = S["pe"].inc(T.matmul(
                        bank[:, 0:1], brow1[0:1, 128 * ic:128 * (ic + 1)],
                        ones0q[0:1, 0:1], start=False, stop=True))
                    S["pe"].wait(C, e_mm)
                    e_q0 = S["act"].inc(C.copy(q0_sb[:, ic:ic + 1], bank[:, 0:1]))
                S["act"].wait(V, e_q0)
                e_q0m = S["dve"].inc(V.memset(q0m[:, :], 0.0))
                S["dve"].wait(V, e_q0m)
                for h in range(NH):
                    e_q0m = S["dve"].inc(V.tensor_copy(
                        q0m[64 * (h % 2):64 * (h % 2) + 64, h:h + 1],
                        q0_sb[64 * (h % 2):64 * (h % 2) + 64, h // 2:h // 2 + 1]))
                qprev = [0] * 8
                for s in range(4):
                    e_strip_mm = 0
                    for ic in range(8):
                        S["act"].wait(T, e_q0 if s == 0 else qprev[ic])
                        for kc in range(8):
                            T.matmul(pb[ic][:, :],
                                     wslot1(kc)[:, 128 * ic:128 * (ic + 1)],
                                     ych(kc)[:, 2 + 512 * s:2 + 512 * (s + 1)],
                                     start=(kc == 0), stop=False)
                        e_strip_mm = S["pe"].inc(T.matmul(
                            pb[ic][:, :], brow1[0:1, 128 * ic:128 * (ic + 1)],
                            ones0q[0:1, 0:512], start=False, stop=True))
                    for ic in range(8):
                        S["pe"].wait(C, e_strip_mm)
                        qprev[ic] = S["act"].inc(C.copy(
                            ych(ic)[:, 512 * s:512 * (s + 1)], pb[ic][:, :]))
                e_q_act = S["act"].n

            def qsl(h, j):
                return ych(h // 2)[64 * (h % 2):64 * (h % 2) + 64,
                                   128 * j:128 * (j + 1)]

            def ksl(h, c0, w):
                return kfm(h // 2)[64 * (h % 2):64 * (h % 2) + 64, c0:c0 + w]

            ev_score, ev_gate, ev_exp, ev_pn, ev_pnwar = {}, {}, {}, {}, {}
            ev_ctx, ev_ctxev, ev_pnTev = {}, {}, {}

            SBK = [0, 1, 4, 5]
            def emit_scores(j, h):
                bank = pb[SBK[h % 4]]
                hh = (j, h - 4) if h >= 4 else (j - 1, h + 12)
                if hh in ev_gate:
                    S["dve"].wait(T, ev_gate[hh])
                    S["act"].wait(T, ev_exp[hh])
                if j == 0 and h in (0, 1):
                    S["act"].wait(T, e_q_act)
                T.matmul(bank[:, 0:132], qsl(h, j), ksl(h, 128 * j, 132),
                         start=True, stop=False)
                ev_score[(j, h)] = S["pe"].inc(
                    T.matmul(bank[:, 132:133], qsl(h, j), ksl(h, 2052, 1),
                             start=False, stop=True))

            def emit_epilogue(j, h):
                bank = pb[SBK[h % 4]]
                buf = h % 4
                bbv = 0 if j == 0 else (2 if j == 15 else 1)
                S["pe"].wait(V, ev_score[(j, h)])
                hh = (j, h - 4) if h >= 4 else (j - 1, h + 12)
                if hh in ev_pnwar:
                    S["pe"].wait(V, ev_pnwar[hh])
                    S["dve"].wait(C, ev_pn[hh])
                V.tensor_add(bank[:, 0:132], bank[:, 0:132],
                             bb_sb[:, 132 * bbv:132 * (bbv + 1)])
                e = S["dve"].inc(V.tensor_add(bank[:, 132:133], bank[:, 132:133],
                                              k0b_sb[:, j:j + 1]))
                S["dve"].wait(V, e)
                ev_gate[(j, h)] = S["dve"].inc(V.tensor_scalar(
                    mf_sb[:, 133 * buf:133 * (buf + 1)], bank[:, 0:133],
                    0.0, None, ALU.is_ge))
                S["dve"].wait(C, ev_gate[(j, h)])
                ev_exp[(j, h)] = S["act"].inc(C.activation(
                    p_sb[:, 133 * buf:133 * (buf + 1)], bank[:, 0:133],
                    AF.Exp, bias=negmf[:, 0:1], scale=1.0))
                S["act"].wait(V, ev_exp[(j, h)])
                e_pg = S["dve"].inc(V.scalar_tensor_tensor(
                    p_sb[:, 133 * buf:133 * (buf + 1)],
                    p_sb[:, 133 * buf:133 * (buf + 1)], 1.0,
                    mf_sb[:, 133 * buf:133 * (buf + 1)],
                    ALU.mult, ALU.mult,
                    accum_out=stats[:, 64 + buf:65 + buf]))
                S["dve"].wait(V, e_pg)
                dcol = dead_sb[:, NH * j + h:NH * j + h + 1]
                e = S["dve"].inc(V.tensor_scalar(dcol, stats[:, 64 + buf:65 + buf],
                                                 0.0, None, ALU.is_le))
                if j == 0:
                    S["dve"].wait(V, e)
                    S["dve"].inc(V.tensor_scalar(dcol, dcol, q0dm_sb[:, 0:1],
                                                 None, ALU.mult))
                e = S["dve"].inc(V.tensor_scalar_max(stats[:, 70 + buf:71 + buf],
                                                     stats[:, 64 + buf:65 + buf],
                                                     1e-30))
                S["dve"].wait(V, e)
                e = S["dve"].inc(V.reciprocal(stats[:, 74 + buf:75 + buf],
                                              stats[:, 70 + buf:71 + buf]))
                S["dve"].wait(V, e)
                ev_pn[(j, h)] = S["dve"].inc(V.tensor_scalar_mul(
                    pn_sb[:, 133 * buf:133 * (buf + 1)],
                    p_sb[:, 133 * buf:133 * (buf + 1)],
                    stats[:, 74 + buf:75 + buf]))
                ev_pnwar[(j, h)] = 0

            def emit_ctx(j, h):
                buf = h % 4
                pT_prev = (j, h - 1) if h >= 1 else (j - 1, 15)
                if pT_prev in ev_pnTev:
                    S["act"].wait(T, ev_pnTev[pT_prev])
                S["dve"].wait(T, ev_pn[(j, h)])
                if j == 0 and h == 0:
                    S["dve"].wait(T, e_vones)
                T.transpose(pb6b[:, 0:128], pn_sb[:, 133 * buf:133 * buf + 128],
                            identb[:, :])
                T.transpose(pb6b[0:4, 128:256],
                            pn_sb[:, 133 * buf + 128:133 * buf + 132],
                            identb[:, :])
                e_tr = S["pe"].inc(T.transpose(
                    pb6b[0:1, 256:384],
                    pn_sb[:, 133 * buf + 132:133 * buf + 133], identb[:, :]))
                ev_pnwar[(j, h)] = e_tr
                S["pe"].wait(C, e_tr)
                C.copy(pnT_sb[:, 0:128], pb6b[:, 0:128])
                C.copy(pnT_sb[0:4, 128:256], pb6b[0:4, 128:256])
                ev_pnTev[(j, h)] = S["act"].inc(
                    C.copy(pnT_sb[0:1, 256:384], pb6b[0:1, 256:384]))
                cb = pb[2 + (h % 2)]
                cprev = (j, h - 2) if h >= 2 else (j - 1, h + 14)
                if cprev in ev_ctxev:
                    S["act"].wait(T, ev_ctxev[cprev])
                S["act"].wait(T, ev_pnTev[(j, h)])
                T.matmul(cb[0:64, 0:128], vtile(j)[:, 64 * h:64 * h + 64],
                         pnT_sb[:, 0:128], start=True, stop=False)
                T.matmul(cb[0:64, 0:128], vtile(j + 1)[0:4, 64 * h:64 * h + 64],
                         pnT_sb[0:4, 128:256], start=False, stop=False)
                ev_ctx[(j, h)] = S["pe"].inc(T.matmul(
                    cb[0:64, 0:128], v0row[0:1, 64 * h:64 * h + 64],
                    pnT_sb[0:1, 256:384], start=False, stop=True))
                S["pe"].wait(C, ev_ctx[(j, h)])
                ev_ctxev[(j, h)] = S["act"].inc(C.copy(
                    ctxfm[64 * (h % 2):64 * (h % 2) + 64,
                          128 * (h // 2):128 * (h // 2) + 128],
                    cb[0:64, 0:128]))

            S["act"].wait(V, e_q_act)
            e_o0ms = S["dve"].inc(V.memset(o0_sb[0:17, :], 0.0))
            e_ones0b = S["dve"].inc(V.memset(ones0b[0:1, :], 1.0))
            S["act"].wait(G, e_q_act)
            e_bo = S["gfet"].inc(G.dma_start(out=brow2[0:1, :], in_=b_d["bo"]), 16)
            e_s0exp_prev = 0
            e_o0g2_prev = 0
            e_o0g13_prev = e_o0ms
            e_oproj_prev = 0
            e_band_last_dve = 0
            for j in range(NBAND):
                for hh0 in range(4):
                    emit_scores(j, hh0)
                for h in range(NH):
                    emit_epilogue(j, h)
                    if h + 4 < NH:
                        emit_scores(j, h + 4)
                    emit_ctx(j, h)
                if not BAND_OPROJ:
                    continue
                S["awo"].wait(T, ev_wo)
                S["gfet"].wait(T, e_bo)
                S["dve"].wait(T, e_ones0b)
                for dh in range(2):
                    S["dve"].wait(T, e_oproj_prev)
                    S["act"].wait(T, ev_ctxev[(j, 15)])
                    for ic in range(8):
                        T.matmul(pb[7][:, :], ctxfm[:, 128 * ic:128 * (ic + 1)],
                                 wslot2(ic)[:, 512 * dh:512 * (dh + 1)],
                                 start=(ic == 0), stop=False)
                    e_mm = S["pe"].inc(T.matmul(
                        pb[7][:, :], ones0b[0:1, 0:128],
                        brow2[0:1, 512 * dh:512 * (dh + 1)],
                        start=False, stop=True))
                    S["pe"].wait(V, e_mm)
                    e_oproj_prev = S["dve"].inc(V.tensor_add(
                        xtile(j)[:, 512 * dh:512 * (dh + 1)],
                        xtile(j)[:, 512 * dh:512 * (dh + 1)], pb[7][:, :]))
                if not BAND_S0:
                    continue
                if S0LVL >= 2:
                    S["act"].wait(T, e_s0exp_prev)
                else:
                    S["dve"].wait(T, e_s0exp_prev)
                S["dve"].wait(T, e_o0g2_prev)
                if j == 0:
                    S["dve"].wait(T, e_q0m)
                S["dve"].wait(T, e_oproj_prev)
                for ic in range(8):
                    last = T.matmul(pb[7][:, 2 * ic:2 * ic + 2],
                                    kfm(ic)[:, 128 * j:128 * (j + 1)],
                                    q0m[:, 2 * ic:2 * ic + 2],
                                    start=(ic == 0), stop=(ic == 7),
                                    skip_group_check=True)
                e_s0 = S["pe"].inc(last)
                s0v = pb[7][:, 0:16]
                S["pe"].wait(V, e_s0)
                if S0LVL == 0:
                    e_mf0 = S["dve"].inc(V.tensor_copy(p0t[:, 0:16], s0v))
                    e_s0exp_prev = e_mf0
                    e_vmc = e_mf0
                    continue
                e = S["dve"].inc(V.tensor_add(
                    s0v, s0v,
                    r0b_sb[:, j:j + 1].broadcast_to([128, 16])))
                S["dve"].wait(V, e)
                e_mf0 = S["dve"].inc(V.tensor_scalar(p0t[:, 0:16], s0v,
                                                     0.0, None, ALU.is_ge))
                e_s0exp_prev = e_mf0
                e_vmc = e_mf0
                if S0LVL >= 2:
                    S["dve"].wait(C, e_mf0)
                    e_s0exp_prev = S["act"].inc(C.activation(
                        p0t[:, 16:32], s0v, AF.Exp, bias=negmf[:, 0:1], scale=1.0))
                if S0LVL >= 3:
                    S["act"].wait(V, e_s0exp_prev)
                    e_p0g = S["dve"].inc(V.scalar_tensor_tensor(
                        p0e[:, 1:17], p0t[:, 16:32], 1.0, p0t[:, 0:16],
                        ALU.mult, ALU.mult))
                    e_vmc = S["dve"].inc(V.tensor_copy(p0e[:, 0:1], vm_sb[:, j:j + 1]))
                if not BAND_O0:
                    continue
                S["dve"].wait(T, e_vmc)
                S["dve"].wait(T, e_o0g13_prev)
                S["act"].wait(T, ev_ctxev[(j, 14)])
                e1 = S["pe"].inc(T.matmul(pb[2][0:17, :], p0e[:, :],
                                          vtile(j)[:, 0:512],
                                          start=True, stop=True))
                S["pe"].wait(V, e1)
                e_add1 = S["dve"].inc(V.tensor_add(o0_sb[0:17, 0:512],
                                                   o0_sb[0:17, 0:512],
                                                   pb[2][0:17, :]))
                S["act"].wait(T, ev_ctxev[(j, 15)])
                e2 = S["pe"].inc(T.matmul(pb[3][0:17, :], p0e[:, :],
                                          vtile(j)[:, 512:1024],
                                          start=True, stop=True))
                S["pe"].wait(V, e2)
                e_o0g2_prev = S["dve"].inc(V.tensor_add(o0_sb[0:17, 512:1024],
                                                        o0_sb[0:17, 512:1024],
                                                        pb[3][0:17, :]))
                S["dve"].wait(T, e_add1)
                e3 = S["pe"].inc(T.matmul(pb[2][0:17, 0:1], p0e[:, :],
                                          vtile(j)[:, 1024:1025],
                                          start=True, stop=True))
                S["pe"].wait(V, e3)
                e_o0g13_prev = S["dve"].inc(V.tensor_add(o0_sb[0:17, 1024:1025],
                                                         o0_sb[0:17, 1024:1025],
                                                         pb[2][0:17, 0:1]))
            e_band_dve = S["dve"].n
            e_band_pe = S["pe"].n
            e_final_dve = e_band_dve

        if STAGE >= 3:
            S["dve"].wait(Y, e_band_dve)
            e_cc2i = S["st"].inc(Y.dma_start(out=cc2i[:, :], in_=o0_sb[:, :]), 16)
            S["st"].wait(G, e_cc2i)
            cci2 = G.collective_compute(
                "AllReduce", ALU.add,
                replica_groups=[[0, 1, 2, 3], [4, 5, 6, 7]],
                ins=[cc2i.ap().opt()], outs=[cc2o.ap().opt()])
            S["cc2d"].inc(cci2, 1)
            G.wait_ge(S["cc2d"].h, 1)
            e_cc2o = S["gfet"].inc(G.dma_start(out=o0_sb[:, :], in_=cc2o[:, :]), 16)

            S["gfet"].wait(V, e_cc2o)
            e = S["dve"].inc(V.tensor_scalar_max(l17[0:17, 0:1],
                                                 o0_sb[0:17, 1024:1025], 1e-30))
            S["dve"].wait(V, e)
            e = S["dve"].inc(V.reciprocal(l17[0:17, 1:2], l17[0:17, 0:1]))
            e_d17 = S["dve"].inc(V.tensor_scalar(l17[0:17, 2:3],
                                                 o0_sb[0:17, 1024:1025],
                                                 0.0, None, ALU.is_le))
            S["dve"].wait(V, e_d17)
            e_o0n = S["dve"].inc(V.tensor_scalar_mul(o0_sb[0:17, :],
                                                     o0_sb[0:17, :],
                                                     l17[0:17, 1:2]))
            S["dve"].wait(V, e_o0n)
            S["pe"].wait(V, e_band_pe)
            e_mv = S["dve"].inc(V.tensor_copy(ctxfm[0:1, 0:1024], o0_sb[0:1, 0:1024]))
            S["dve"].wait(T, e_d17)
            S["dve"].wait(T, e_oproj_prev)
            e_tr = S["pe"].inc(T.transpose(pb[7][0:1, 0:17], l17[0:17, 2:3],
                                           identf[0:17, 0:17]))
            S["pe"].wait(C, e_tr)
            e_dr = S["act"].inc(C.copy(deadr[0:1, 0:17], pb[7][0:1, 0:17]))
            e_prev_add = e_mv
            for half in range(2):
                S["act"].wait(T, e_dr)
                S["dve"].wait(T, e_prev_add)
                e_mm = S["pe"].inc(T.matmul(
                    pb[7][0:17, :], deadr[0:1, 0:17],
                    ctxfm[0:1, 512 * half:512 * (half + 1)],
                    start=True, stop=True))
                S["pe"].wait(V, e_mm)
                e_prev_add = S["dve"].inc(V.tensor_add(
                    o0_sb[0:17, 512 * half:512 * (half + 1)],
                    o0_sb[0:17, 512 * half:512 * (half + 1)], pb[7][0:17, :]))
            e_o0f = e_prev_add

            e_wovm_mm = 0
            e_pb4_ev = 0
            e_cx = 0
            e_mvs = 0
            for ic in range(8):
                S["dve"].wait(T, e_o0f)
                S["act"].wait(T, e_pb4_ev)
                S["dve"].wait(T, e_mvs)
                e_tr = S["pe"].inc(T.transpose(
                    pb[4][:, 0:17], o0_sb[0:17, 128 * ic:128 * (ic + 1)],
                    identf[0:17, 0:17]))
                S["pe"].wait(C, e_tr)
                C.copy(ctx0[0:64, ic:ic + 1], pb[4][0:64, 1 + 2 * ic:2 + 2 * ic])
                e_cx = S["act"].inc(C.copy(ctx0[64:128, ic:ic + 1],
                                           pb[4][64:128, 2 + 2 * ic:3 + 2 * ic]))
                e_pb4_ev = e_cx
                S["pe"].wait(V, e_tr)
                S["pe"].wait(V, e_wovm_mm)
                e_ms = S["dve"].inc(V.memset(mvsel[:, :], 0.0))
                S["dve"].wait(V, e_ms)
                V.tensor_copy(mvsel[0:64, 2 * ic:2 * ic + 1], pb[4][0:64, 0:1])
                e_mvs = S["dve"].inc(V.tensor_copy(
                    mvsel[64:128, 2 * ic + 1:2 * ic + 2], pb[4][64:128, 0:1]))
                S["dve"].wait(T, e_mvs)
                T.matmul(pb[2][0:16, :], mvsel[:, :], wslot2(ic)[:, 0:512],
                         start=(ic == 0), stop=(ic == 7))
                e_wovm_mm = S["pe"].inc(T.matmul(
                    pb[3][0:16, :], mvsel[:, :], wslot2(ic)[:, 512:1024],
                    start=(ic == 0), stop=(ic == 7)))
            e_prev = e_o0f
            for dh in range(2):
                S["act"].wait(T, e_cx)
                S["dve"].wait(T, e_prev)
                for ic in range(8):
                    ins = T.matmul(pb[7][0:1, :], ctx0[:, ic:ic + 1],
                                   wslot2(ic)[:, 512 * dh:512 * (dh + 1)],
                                   start=(ic == 0), stop=(ic == 7))
                e_mm = S["pe"].inc(ins)
                S["pe"].wait(V, e_mm)
                e_prev = S["dve"].inc(V.scalar_tensor_tensor(
                    x_sb[0:1, 512 * dh:512 * (dh + 1)], pb[7][0:1, :],
                    ownm_sb[0:1, 0:1], x_sb[0:1, 512 * dh:512 * (dh + 1)],
                    ALU.mult, ALU.add))
            e_rowfix = e_prev

            S["pe"].wait(C, S["pe"].n)
            C.copy(wovm[:, 0:512], pb[2][0:16, :])
            e_wovm = S["act"].inc(C.copy(wovm[:, 512:1024], pb[3][0:16, :]))

            e_dt = 0
            e_fixadd = e_rowfix
            for t in range(NT):
                S["act"].wait(T, e_dt)
                S["dve"].wait(T, e_band_dve)
                e_tr = S["pe"].inc(T.transpose(
                    pb6b[0:16, 0:128], dead_sb[:, NH * t:NH * (t + 1)],
                    identb[:, :]))
                S["pe"].wait(C, e_tr)
                e_dt = S["act"].inc(C.copy(deadt[0:16, :], pb6b[0:16, 0:128]))
                for dh in range(2):
                    S["act"].wait(T, e_dt)
                    S["act"].wait(T, e_wovm)
                    S["dve"].wait(T, e_fixadd)
                    e_mm = S["pe"].inc(T.matmul(
                        pb[5][:, :], deadt[0:16, :],
                        wovm[:, 512 * dh:512 * (dh + 1)],
                        start=True, stop=True))
                    S["pe"].wait(V, e_mm)
                    e_fixadd = S["dve"].inc(V.tensor_add(
                        xtile(t)[:, 512 * dh:512 * (dh + 1)],
                        xtile(t)[:, 512 * dh:512 * (dh + 1)], pb[5][:, :]))
            e_combine_dve = e_fixadd
            e_combine_pe = S["pe"].n

            e_ffn2, _ = emit_ffn(False, e_combine_dve, e_band_pe, e_combine_pe)
            e_final_dve = e_ffn2

        S["dve"].wait(Y, e_final_dve)
        xo = xout.rearrange("(t p) d -> p t d", p=128)
        for i in range(4):
            S["st"].inc(Y.dma_start(out=xo[:, 4 * i:4 * i + 4, :],
                                    in_=xv[:, 4 * i:4 * i + 4, :]), 16)
        Y.wait_ge(S["st"].h, S["st"].n)

        for nm, px in (("tensor", T), ("vector", V), ("scalar", C),
                       ("gpsimd", G), ("sync", Y)):
            getattr(block, nm)(px.flush)

    return nc


_CACHE = {}


def kernel(L0_attn_x, L0_attn_wq, L0_attn_bq, L0_attn_wk, L0_attn_bk,
           L0_attn_wv, L0_attn_bv, L0_attn_wo, L0_attn_bo,
           L0_ffn_router_kernel_DE, L0_ffn_kernel_gating_EDF,
           L0_ffn_kernel_up_EDF, L0_ffn_kernel_down_EFD):
    x = np.ascontiguousarray(np.asarray(L0_attn_x, dtype=np.float32))
    shared = dict(
        wqT=np.ascontiguousarray(np.asarray(L0_attn_wq, np.float32).T),
        wkT=np.ascontiguousarray(np.asarray(L0_attn_wk, np.float32).T),
        wvT=np.ascontiguousarray(np.asarray(L0_attn_wv, np.float32).T),
        woT=np.ascontiguousarray(np.asarray(L0_attn_wo, np.float32).T),
        bq=np.asarray(L0_attn_bq, np.float32).reshape(1, DM),
        bk=np.asarray(L0_attn_bk, np.float32).reshape(1, DM),
        bv=np.asarray(L0_attn_bv, np.float32).reshape(1, DM),
        bo=np.asarray(L0_attn_bo, np.float32).reshape(1, DM),
        router=np.ascontiguousarray(np.asarray(L0_ffn_router_kernel_DE, np.float32)),
        wg=np.ascontiguousarray(np.asarray(L0_ffn_kernel_gating_EDF, np.float32)),
        wu=np.ascontiguousarray(np.asarray(L0_ffn_kernel_up_EDF, np.float32)),
        wd=np.ascontiguousarray(np.asarray(L0_ffn_kernel_down_EFD, np.float32)),
        identb=np.eye(128, dtype=ml_dtypes.bfloat16),
        identf=np.eye(128, dtype=np.float32),
    )
    if "nc" not in _CACHE:
        _CACHE["nc"] = build_nc()
    nc = _CACHE["nc"]
    in_maps = []
    for core in range(8):
        b, rank = divmod(core, 4)
        m = dict(shared)
        m["xin"] = np.ascontiguousarray(x[b, rank * TPC:(rank + 1) * TPC, :])
        m.update(_host_consts(rank))
        in_maps.append(m)
    res = run_bass_kernel_spmd(nc, in_maps, core_ids=list(range(8)))
    out = np.empty((2, 8192, DM), dtype=np.float32)
    for core in range(8):
        b, rank = divmod(core, 4)
        out[b, rank * TPC:(rank + 1) * TPC, :] = res.results[core]["xout"]
    return out


if __name__ == "__main__":
    nc = build_nc()
    print("build ok")


# version 04
# speedup vs baseline: 1.0563x; 1.0563x over previous
import numpy as np
import ml_dtypes
from contextlib import ExitStack

import concourse.bass as bass
import concourse.mybir as mybir
from concourse.bass_utils import run_bass_kernel_spmd

F32 = mybir.dt.float32
BF16 = mybir.dt.bfloat16
AF = mybir.ActivationFunctionType
ALU = mybir.AluOpType
AX = mybir.AxisListType

NT = 16
TPC = 2048
DM = 1024
NH = 16
NE = 8
DF = 2048
MASKV = -3.0e9
MFIX = 24.0
YW = 2053

STAGE = 3
ATT_GATHER = True
ATT_LN2 = True
ATT_EXTRAS = True
ATT_KVQ = True
NBAND = 16
BAND_OPROJ = True
BAND_S0 = True
BAND_O0 = True
S0LVL = 3
PHASE2 = True
NEXP = 8


def _pos_bias(d):
    d = np.asarray(d, dtype=np.float64)
    return (np.exp(-0.5 * (d / 2.0) ** 2)
            + np.cos(2.0 * np.pi * d / 401.0)).astype(np.float32)


def _host_consts(rank):
    a = rank * TPC
    pcol = np.arange(128)
    c = np.arange(132)[None, :]
    dist = pcol[:, None] - c + 2
    base = np.where(np.abs(dist) <= 2, _pos_bias(dist), MASKV).astype(np.float32)
    bb = np.stack([base.copy(), base.copy(), base.copy()])
    if rank == 0:
        bb[0][:, 0:2] = MASKV
        bb[0][:, 2] = _pos_bias(pcol)
        bb[0][0, :] = MASKV
    if rank == 3:
        bb[2][:, 130:132] = MASKV
    k0b = np.empty((128, NT), dtype=np.float32)
    for j in range(NT):
        k0b[:, j] = _pos_bias(a + 128 * j + pcol)
    if rank == 0:
        k0b[:, 0] = MASKV
    r0b = np.full((128, 17), MASKV, dtype=np.float32)
    vm = np.zeros((128, 17), dtype=np.float32)
    for j in range(17):
        kl = 128 * j + pcol - 2
        valid = (kl >= 0) & (kl < TPC)
        r0b[valid, j] = _pos_bias(a + kl[valid])
        vm[valid, j] = 1.0
    sel = np.zeros((16, 5), dtype=np.float32)
    if rank > 0:
        sel[4 * (rank - 1) + 2, 0] = 1.0
        sel[4 * (rank - 1) + 3, 1] = 1.0
    if rank < 3:
        sel[4 * (rank + 1) + 0, 2] = 1.0
        sel[4 * (rank + 1) + 1, 3] = 1.0
    sel[0, 4] = 1.0
    q0dm = np.ones((128, 1), dtype=np.float32)
    if rank == 0:
        q0dm[0, 0] = 0.0
    ownm = np.full((1, 1), 1.0 if rank == 0 else 0.0, dtype=np.float32)
    return dict(bb=bb.reshape(3 * 128, 132), k0b=k0b, r0b=r0b, vm=vm,
                selm=sel.astype(ml_dtypes.bfloat16), q0dm=q0dm, ownm=ownm)


class _Pend:
    __slots__ = ("op", "args", "kwargs", "incs")


class EngProxy:

    def __init__(self, name):
        self._name = name
        self._q = []

    def __getattr__(self, op):
        def call(*a, **k):
            h = _Pend()
            h.op, h.args, h.kwargs, h.incs = op, a, k, []
            self._q.append(h)
            return h
        return call

    def flush(self, e):
        for h in self._q:
            ins = getattr(e, h.op)(*h.args, **h.kwargs)
            for s, k in h.incs:
                ins.then_inc(s, k)


class Sem:
    __slots__ = ("h", "n")

    def __init__(self, h):
        self.h = h
        self.n = 0

    def inc(self, ins, k=1):
        ins.incs.append((self.h, k))
        self.n += k
        return self.n

    def wait(self, eng, val):
        assert val <= self.n, "wait on future semaphore value"
        if val > 0:
            eng.wait_ge(self.h, val)


def build_nc():
    nc = bass.Bass("TRN2", target_bir_lowering=False)

    def din(name, shape, dt=F32):
        return nc.dram_tensor(name, shape, dt, kind="ExternalInput").ap()

    xin = din("xin", [TPC, DM])
    wT_d = {w: din(w, [DM, DM]) for w in ("wqT", "wkT", "wvT", "woT")}
    b_d = {b: din(b, [1, DM]) for b in ("bq", "bk", "bv", "bo")}
    router_d = din("router", [DM, NE])
    wg_d = din("wg", [NE, DM, DF])
    wu_d = din("wu", [NE, DM, DF])
    wd_d = din("wd", [NE, DF, DM])
    bb_d = din("bb", [3 * 128, 132])
    k0b_d = din("k0b", [128, NT])
    r0b_d = din("r0b", [128, 17])
    vm_d = din("vm", [128, 17])
    selm_d = din("selm", [16, 5], BF16)
    q0dm_d = din("q0dm", [128, 1])
    ownm_d = din("ownm", [1, 1])
    identb_d = din("identb", [128, 128], BF16)
    identf_d = din("identf", [128, 128], F32)

    xout = nc.dram_tensor("xout", [TPC, DM], F32, kind="ExternalOutput").ap()

    wgbf = nc.dram_tensor("wgbf", [NE, DM, DF], BF16)
    wubf = nc.dram_tensor("wubf", [NE, DM, DF], BF16)
    cc1i = nc.dram_tensor("cc1i", [4, DM], F32)
    cc1o = nc.dram_tensor("cc1o", [16, DM], F32)
    cc2i = nc.dram_tensor("cc2i", [17, 1025], F32)
    cc2o = nc.dram_tensor("cc2o", [17, 1025], F32)

    with ExitStack() as ctx:
        def sbt(name, shape, dt):
            return ctx.enter_context(nc.sbuf_tensor(name, shape, dt))

        x_sb = sbt("x_sb", [128, NT * DM], F32)
        arenaA = sbt("arenaA", [128, 33856], BF16)
        y2q = sbt("y2q", [128, 8 * YW], BF16)
        arenaB = sbt("arenaB", [128, 16384], BF16)
        stats = sbt("stats", [128, 80], F32)
        fw_sb = sbt("fw_sb", [128, NT * NE], F32)
        lg_sb = sbt("lg_sb", [128, NT * NE], F32)
        m2_sb = sbt("m2_sb", [128, NT * NE], BF16)
        rt_sb = sbt("rt_sb", [128, 48], F32)
        rt_w = sbt("rt_w", [128, 8 * NE], F32)
        bb_sb = sbt("bb_sb", [128, 3 * 132], F32)
        k0b_sb = sbt("k0b_sb", [128, NT], F32)
        r0b_sb = sbt("r0b_sb", [128, 17], F32)
        vm_sb = sbt("vm_sb", [128, 17], F32)
        q0dm_sb = sbt("q0dm_sb", [128, 1], F32)
        ownm_sb = sbt("ownm_sb", [1, 1], F32)
        identb = sbt("identb_s", [128, 128], BF16)
        identf = sbt("identf_s", [128, 128], F32)
        pnT_sb = sbt("pnT_sb", [128, 384], BF16)
        ctxfm = sbt("ctxfm", [128, 1024], BF16)
        dead_sb = sbt("dead_sb", [128, NT * NH], BF16)
        p0e = sbt("p0e", [128, 17], BF16)
        p0t = sbt("p0t", [128, 32], F32)
        q0_sb = sbt("q0_sb", [128, 8], BF16)
        q0m = sbt("q0m", [128, 16], BF16)
        ctx0 = sbt("ctx0", [128, 8], BF16)
        mvsel = sbt("mvsel", [128, 16], BF16)
        brow1 = sbt("brow1", [65, DM], BF16)
        gthr = None
        selm_sb = sbt("selm_sb", [16, 5], BF16)
        deadr = sbt("deadr", [1, 17], BF16)
        mvrow = None
        l17 = sbt("l17", [17, 4], F32)
        v0row_t = sbt("v0row", [1, 1025], BF16)
        negmf = sbt("negmf", [128, 1], F32)

        v0row = vtile_slot16 = None
        pb = [ctx.enter_context(nc.psum_tensor(f"pb{i}", [128, 512], F32))
              for i in range(8)]
        pb6b = pb[6][:, :].bitcast(BF16)
        _stg = arenaB[:, 12288:16384].bitcast(F32)
        _wv1f = arenaB[:, 0:8192].bitcast(F32)
        sstage = _stg[:, 0:1024]
        ytm = _stg[:, 0:1024]
        yfev = _stg[:, 1536:1792]
        o0_sb = _wv1f[0:17, 0:1025]
        p_sb = _wv1f[:, 1030:1562]
        bias133 = _wv1f[:, 2880:3013]
        mf_sb = arenaB[:, 3124:3656]
        pn_sb = arenaB[:, 3656:4188]

        S = {n: Sem(ctx.enter_context(nc.semaphore(n)))
             for n in ["ldx0", "ldx1", "ldx2", "ldx3", "ldc", "ldg", "wld", "wdld0", "wdld1", "wdld2",
                       "wdld3", "awk", "awv", "awq", "awo", "gfet", "wldb", "pe", "act",
                       "dve", "cc1d", "cc2d", "st"] +
                      [f"cvt{i}" for i in range(8)]}

        block = ctx.enter_context(nc.Block())
        T, V, C, G, Y = (EngProxy("tensor"), EngProxy("vector"),
                         EngProxy("scalar"), EngProxy("gpsimd"),
                         EngProxy("sync"))

        def wgc(kc):
            return arenaA[:, 2048 * kc:2048 * (kc + 1)]

        def wuc(kc):
            return arenaA[:, 16384 + 2048 * kc:16384 + 2048 * (kc + 1)]

        def kfm(ic):
            return arenaA[:, YW * ic:YW * (ic + 1)]

        def vtile(j):
            return arenaA[:, 16424 + 1025 * j:16424 + 1025 * (j + 1)]

        def ych(kc):
            return y2q[:, YW * kc:YW * (kc + 1)]

        def fch(fc):
            return arenaB[:, 512 * fc:512 * (fc + 1)]

        def wdch(slot):
            return arenaB[:, 8192 + 1024 * slot:8192 + 1024 * (slot + 1)]

        def wslot1(kc):
            return arenaB[:, 1024 * kc:1024 * (kc + 1)]

        def wslot2(kc):
            return arenaB[:, 8192 + 1024 * kc:8192 + 1024 * (kc + 1)]

        def xtile(t):
            return x_sb[:, DM * t:DM * (t + 1)]

        ones64 = vtile(16)[64:65, 0:512]
        v0row = v0row_t[0:1, :]
        ones32 = ctxfm[32:33, 0:512]
        deadt = pn_sb[0:16, 0:128]
        ones0q = ctxfm[0:1, 512:1024]
        ones0b = arenaB[0:1, 4224:4736]
        brow2 = arenaB[0:1, 4736:5760]
        gthr = ctxfm[0:16, :]
        wovm = arenaB[0:16, 12288:13312]
        x1ex = ctxfm[0:5, :]

        xr = xin.rearrange("(t p) d -> p t d", p=128)
        xv = x_sb[:, :].rearrange("p (t d) -> p t d", d=DM)
        for i in range(4):
            S[f"ldx{i}"].inc(Y.dma_start(out=xv[:, 4 * i:4 * i + 4, :],
                                        in_=xr[:, 4 * i:4 * i + 4, :]), 16)
        e_ldx = 16
        for dst, src in [(bb_sb[:, :].rearrange("p (v c) -> p v c", v=3),
                          bb_d.rearrange("(v p) c -> p v c", p=128)),
                         (k0b_sb, k0b_d), (r0b_sb, r0b_d), (vm_sb, vm_d),
                         (q0dm_sb, q0dm_d), (ownm_sb, ownm_d),
                         (identb, identb_d), (identf, identf_d),
                         (selm_sb, selm_d)]:
            S["ldc"].inc(Y.dma_start(
                out=dst if isinstance(dst, bass.AP) else dst[:, :],
                in_=src), 16)
        for kc in range(8):
            S["ldc"].inc(Y.dma_start(out=rt_w[:, 8 * kc:8 * kc + 8],
                                     in_=router_d[128 * kc:128 * kc + 128, :]), 16)
        S["ldg"].inc(G.dma_start(out=brow1[0:1, :], in_=b_d["bq"]), 16)
        S["ldg"].inc(G.dma_start(out=brow1[32:33, :], in_=b_d["bk"]), 16)
        S["ldg"].inc(G.dma_start(out=brow1[64:65, :], in_=b_d["bv"]), 16)
        e_ldc = S["ldc"].n
        e_ldg = S["ldg"].n
        V.memset(ones64[:, :], 1.0)
        V.memset(negmf[:, :], -MFIX)

        wd_state = {"seq": 0, "evdma": {}, "evfree": {}}
        conv_order = [2, 3, 4, 5, 6, 7, 0, 1]
        cvt_todo = [(e, w, kc) for e in conv_order for w in (0, 1)
                    for kc in range(8)]

        def emit_ln_sweep(e_x_dve, first=False):
            S["dve"].wait(V, e_x_dve)
            ev = 0
            for t in range(NT):
                if first and t % 4 == 0:
                    S[f"ldx{t // 4}"].wait(V, 16)
                V.bn_stats(stats[:, 64:70], xtile(t)[:, 0:512])
                e = S["dve"].inc(V.bn_stats(stats[:, 70:76], xtile(t)[:, 512:1024]))
                S["dve"].wait(V, e)
                e = S["dve"].inc(V.bn_aggr(stats[:, 76:78], stats[:, 64:76]))
                S["dve"].wait(V, e)
                V.tensor_copy(stats[:, t:t + 1], stats[:, 76:77])
                ev = S["dve"].inc(V.tensor_scalar_add(stats[:, 32 + t:33 + t],
                                                      stats[:, 77:78], 1e-5))
            S["dve"].wait(C, ev)
            e_sig = S["act"].inc(C.activation(stats[:, 48:64], stats[:, 32:48],
                                              AF.Sqrt))
            S["act"].wait(V, e_sig)
            return S["dve"].inc(V.reciprocal(stats[:, 16:32], stats[:, 48:64]))

        def emit_ffn(first_call, e_x_dve, e_arenaA_pe, e_arenaB_pe):
            e_rstd = emit_ln_sweep(e_x_dve, first=first_call)

            ev_lg = {}
            prev_cast = [0, 0]
            prev_mm = [0, 0]
            S["pe"].wait(V, e_arenaB_pe)
            for t in range(NT):
                S["dve"].wait(V, e_rstd)
                e_ytm = S["dve"].inc(V.tensor_scalar(
                    ytm[:, :], xtile(t), stats[:, t:t + 1], stats[:, 16 + t:17 + t],
                    ALU.subtract, ALU.mult))
                S["dve"].wait(T, e_ytm)
                if t == 0:
                    S["ldc"].wait(T, e_ldc)
                for kc in range(8):
                    buf = kc % 2
                    S["dve"].wait(T, prev_cast[buf])
                    S["pe"].wait(T, prev_mm[buf])
                    e_tr = S["pe"].inc(T.transpose(pb[4 + buf][:, 0:128],
                                                   ytm[:, 128 * kc:128 * (kc + 1)],
                                                   identf[:, :]))
                    S["pe"].wait(C, e_tr)
                    e_ev = S["act"].inc(C.copy(yfev[:, 128 * buf:128 * (buf + 1)],
                                               pb[4 + buf][:, 0:128]))
                    S["act"].wait(V, e_ev)
                    prev_cast[buf] = S["dve"].inc(V.tensor_copy(
                        ych(kc)[:, 128 * t:128 * (t + 1)],
                        yfev[:, 128 * buf:128 * (buf + 1)]))
                    S["act"].wait(T, e_ev)
                    prev_mm[buf] = S["pe"].inc(T.matmul(
                        pb[6][:, 0:8], yfev[:, 128 * buf:128 * (buf + 1)],
                        rt_w[:, 8 * kc:8 * kc + 8],
                        start=(kc == 0), stop=(kc == 7)))
                S["pe"].wait(V, prev_mm[1])
                ev_lg[t] = S["dve"].inc(V.tensor_copy(lg_sb[:, 8 * t:8 * t + 8],
                                                      pb[6][:, 0:8]))
            e_ycast = S["dve"].n

            e_stash = 0
            for t in range(NT):
                L = lg_sb[:, 8 * t:8 * t + 8]
                S["dve"].wait(V, ev_lg[t])
                e = S["dve"].inc(V.reduce_max(rt_sb[:, 0:1], L, axis=AX.X))
                S["dve"].wait(V, e)
                e = S["dve"].inc(V.tensor_scalar(rt_sb[:, 2:10], L, rt_sb[:, 0:1],
                                                 None, ALU.is_ge))
                S["dve"].wait(V, e)
                e = S["dve"].inc(V.scalar_tensor_tensor(
                    rt_sb[:, 10:18], rt_sb[:, 2:10], -1.0e30, L, ALU.mult, ALU.add))
                S["dve"].wait(V, e)
                e = S["dve"].inc(V.reduce_max(rt_sb[:, 1:2], rt_sb[:, 10:18],
                                              axis=AX.X))
                S["dve"].wait(V, e)
                e = S["dve"].inc(V.tensor_scalar(rt_sb[:, 18:26], rt_sb[:, 10:18],
                                                 rt_sb[:, 1:2], None, ALU.is_ge))
                ng = S["dve"].inc(V.tensor_scalar_mul(rt_sb[:, 40:41],
                                                      rt_sb[:, 0:1], -1.0))
                S["dve"].wait(C, ng)
                e_exp = S["act"].inc(C.activation(rt_sb[:, 26:34], L, AF.Exp,
                                                  bias=rt_sb[:, 40:41], scale=1.0,
                                                  accum_out=rt_sb[:, 34:35]))
                S["act"].wait(V, e_exp)
                e = S["dve"].inc(V.tensor_mul(rt_sb[:, 10:18], rt_sb[:, 26:34],
                                              rt_sb[:, 2:10]))
                S["dve"].wait(V, e)
                e = S["dve"].inc(V.reduce_sum(rt_sb[:, 35:36], rt_sb[:, 10:18],
                                              axis=AX.X))
                S["dve"].wait(V, e)
                e = S["dve"].inc(V.tensor_mul(rt_sb[:, 10:18], rt_sb[:, 26:34],
                                              rt_sb[:, 18:26]))
                S["dve"].wait(V, e)
                e = S["dve"].inc(V.reduce_sum(rt_sb[:, 36:37], rt_sb[:, 10:18],
                                              axis=AX.X))
                S["dve"].wait(V, e)
                S["dve"].inc(V.tensor_sub(rt_sb[:, 37:38], rt_sb[:, 35:36],
                                          rt_sb[:, 36:37]))
                e = S["dve"].inc(V.reciprocal(rt_sb[:, 38:39], rt_sb[:, 34:35]))
                S["dve"].wait(V, e)
                e = S["dve"].inc(V.tensor_mul(rt_sb[:, 39:40], rt_sb[:, 37:38],
                                              rt_sb[:, 38:39]))
                S["dve"].wait(V, e)
                S["dve"].inc(V.tensor_copy(fw_sb[:, 8 * t:8 * t + 1],
                                           rt_sb[:, 39:40]))
                S["dve"].inc(V.tensor_copy(L, rt_sb[:, 2:10]))
                e_stash = S["dve"].inc(V.tensor_copy(m2_sb[:, 8 * t:8 * t + 8],
                                                     rt_sb[:, 18:26]))
            argall = fw_sb[:, :].rearrange("p (t e) -> p t e", e=NE)[:, :, 0]
            S["dve"].wait(C, e_stash)
            e_sg = S["act"].inc(C.activation(stats[:, 48:64], argall, AF.Sigmoid))
            S["act"].wait(V, e_sg)
            e_fw = 0
            for t in range(NT):
                nw1 = stats[:, 48 + t:49 + t]
                S["dve"].wait(V, e_fw if t > 0 else 0)
                S["dve"].inc(V.tensor_scalar(rt_sb[:, 2:10],
                                             lg_sb[:, 8 * t:8 * t + 8],
                                             nw1, 0.5, ALU.mult, ALU.mult))
                e = S["dve"].inc(V.tensor_scalar(rt_sb[:, 10:18],
                                                 m2_sb[:, 8 * t:8 * t + 8],
                                                 nw1, 0.5, ALU.mult, ALU.mult))
                S["dve"].wait(V, e)
                e = S["dve"].inc(V.scalar_tensor_tensor(
                    rt_sb[:, 10:18], m2_sb[:, 8 * t:8 * t + 8], 0.5,
                    rt_sb[:, 10:18], ALU.mult, ALU.subtract))
                S["dve"].wait(V, e)
                e_fw = S["dve"].inc(V.tensor_add(fw_sb[:, 8 * t:8 * t + 8],
                                                 rt_sb[:, 2:10], rt_sb[:, 10:18]))

            def load_wgu(e_idx, wait_pe):
                staged = (not first_call) or e_idx >= 2
                if not staged:
                    S["pe"].wait(G, wait_pe)
                    for kc in range(8):
                        S["wld"].inc(G.dma_start(
                            out=wgc(kc),
                            in_=wg_d[e_idx, 128 * kc:128 * (kc + 1), :]), 16)
                    for kc in range(8):
                        S["wld"].inc(G.dma_start(
                            out=wuc(kc),
                            in_=wu_d[e_idx, 128 * kc:128 * (kc + 1), :]), 16)
                    return ("wld", S["wld"].n)
                S["pe"].wait(Y, wait_pe)
                assert S[f"cvt{e_idx}"].n == 256, "conversion not fully emitted"
                S[f"cvt{e_idx}"].wait(Y, 256)
                for kc in range(8):
                    S["wldb"].inc(Y.dma_start(
                        out=wgc(kc), in_=wgbf[e_idx, 128 * kc:128 * (kc + 1), :]), 16)
                for kc in range(8):
                    S["wldb"].inc(Y.dma_start(
                        out=wuc(kc), in_=wubf[e_idx, 128 * kc:128 * (kc + 1), :]), 16)
                return ("wldb", S["wldb"].n)

            wd_plan = [(e_i, fc) for e_i in range(NEXP) for _s in range(4)
                       for fc in range(16)]
            base_seq = wd_state["seq"]

            def load_wd_next(i_local, wait_pe_extra=0):
                seq = wd_state["seq"]
                e_i, fc = wd_plan[i_local]
                if wait_pe_extra:
                    S["pe"].wait(G, wait_pe_extra)
                if seq >= 4:
                    S["pe"].wait(G, wd_state["evfree"][seq - 4])
                wd_state["evdma"][seq] = S[f"wdld{seq % 4}"].inc(G.dma_start(
                    out=wdch(seq % 4),
                    in_=wd_d[e_i, 128 * fc:128 * (fc + 1), :]), 16)
                wd_state["seq"] += 1
                if first_call and cvt_todo:
                    ce, cw, ckc = cvt_todo.pop(0)
                    dsrc = wg_d if cw == 0 else wu_d
                    ddst = wgbf if cw == 0 else wubf
                    S[f"cvt{ce}"].inc(G.dma_start(
                        out=ddst[ce, 128 * ckc:128 * (ckc + 1), :],
                        in_=dsrc[ce, 128 * ckc:128 * (ckc + 1), :]), 16)

            if NEXP > 0:
                ev_wgu = load_wgu(0, e_arenaA_pe)
                for i in range(min(4, len(wd_plan))):
                    load_wd_next(i, e_arenaB_pe if i == 0 else 0)
            n_wd_loaded = min(4, len(wd_plan))

            silu_free = [0, 0]
            fmul_free = [0, 0]
            ev_fmul_fc = {}
            ev_xupd = [0] * 8
            e_p1_last = 0
            e_phase2_read = e_arenaB_pe
            for ex in range(NEXP):
                S[ev_wgu[0]].wait(T, ev_wgu[1])
                if ex == 0:
                    S["dve"].wait(T, e_fw)
                    S["dve"].wait(T, e_ycast)
                for s in range(4):
                    for fc in range(16):
                        gb, ub = pb[fc % 2], pb[2 + fc % 2]
                        S["act"].wait(T, silu_free[fc % 2])
                        S["dve"].wait(T, fmul_free[fc % 2])
                        if fc < 2:
                            S["dve"].wait(T, ev_xupd[fc % 2])
                            S["dve"].wait(T, ev_xupd[2 + fc % 2])
                        if fc == 0:
                            S["pe"].wait(V, e_phase2_read)
                        for kc in range(8):
                            ins = T.matmul(gb[:, :],
                                           wgc(kc)[:, 128 * fc:128 * (fc + 1)],
                                           ych(kc)[:, 512 * s:512 * (s + 1)],
                                           start=(kc == 0), stop=(kc == 7))
                        eg = S["pe"].inc(ins)
                        for kc in range(8):
                            ins = T.matmul(ub[:, :],
                                           wuc(kc)[:, 128 * fc:128 * (fc + 1)],
                                           ych(kc)[:, 512 * s:512 * (s + 1)],
                                           start=(kc == 0), stop=(kc == 7))
                        eu = S["pe"].inc(ins)
                        e_p1_last = eu
                        S["pe"].wait(C, eg)
                        silu_free[fc % 2] = S["act"].inc(C.activation(
                            sstage[:, 512 * (fc % 2):512 * (fc % 2 + 1)],
                            gb[:, :], AF.Silu))
                        S["pe"].wait(V, eu)
                        S["act"].wait(V, silu_free[fc % 2])
                        fmul_free[fc % 2] = S["dve"].inc(V.tensor_mul(
                            fch(fc),
                            sstage[:, 512 * (fc % 2):512 * (fc % 2 + 1)],
                            ub[:, :]))
                        ev_fmul_fc[fc] = fmul_free[fc % 2]
                    if not PHASE2:
                        continue
                    for k in range(8):
                        S["dve"].wait(T, ev_xupd[k])
                    S["act"].wait(T, silu_free[0])
                    S["act"].wait(T, silu_free[1])
                    S["dve"].wait(T, fmul_free[0])
                    S["dve"].wait(T, fmul_free[1])
                    for fc in range(16):
                        seq = base_seq + (ex * 4 + s) * 16 + fc
                        S[f"wdld{seq % 4}"].wait(T, wd_state["evdma"][seq])
                        S["dve"].wait(T, ev_fmul_fc[fc])
                        last = None
                        for tt in range(4):
                            for dh in range(2):
                                last = T.matmul(
                                    pb[tt * 2 + dh][:, :],
                                    fch(fc)[:, 128 * tt:128 * (tt + 1)],
                                    wdch(seq % 4)[:, 512 * dh:512 * (dh + 1)],
                                    start=(fc == 0), stop=(fc == 15))
                        wd_state["evfree"][seq] = S["pe"].inc(last)
                        if n_wd_loaded < len(wd_plan):
                            load_wd_next(n_wd_loaded)
                            n_wd_loaded += 1
                    e_p2 = S["pe"].n
                    if s == 3 and ex + 1 < NEXP:
                        ev_wgu = load_wgu(ex + 1, e_p1_last)
                    S["dve"].wait(V, e_fw)
                    for tt in range(4):
                        t_glob = 4 * s + tt
                        for dh in range(2):
                            S["pe"].wait(V, e_p2)
                            ev_xupd[tt * 2 + dh] = S["dve"].inc(
                                V.scalar_tensor_tensor(
                                    xtile(t_glob)[:, 512 * dh:512 * (dh + 1)],
                                    pb[tt * 2 + dh][:, :],
                                    fw_sb[:, 8 * t_glob + ex:8 * t_glob + ex + 1],
                                    xtile(t_glob)[:, 512 * dh:512 * (dh + 1)],
                                    ALU.mult, ALU.add))
                    e_phase2_read = e_p2
            return S["dve"].n, S["pe"].n

        e_ffn1, e_ffn1_pe = emit_ffn(True, 0, 0, 0)
        e_final_dve = e_ffn1

        if STAGE >= 2:
            e_gthr = 0
            if ATT_GATHER:
                S["dve"].wait(Y, e_ffn1)
                S["st"].inc(Y.dma_start(out=cc1i[0:2, :], in_=x_sb[0:2, 0:DM]), 16)
                e_cc1i = S["st"].inc(Y.dma_start(
                    out=cc1i[2:4, :], in_=x_sb[126:128, DM * 15:DM * 16]), 16)
                S["st"].wait(G, e_cc1i)
                cci = G.collective_compute(
                    "AllGather", ALU.bypass,
                    replica_groups=[[0, 1, 2, 3], [4, 5, 6, 7]],
                    ins=[cc1i.ap().opt()], outs=[cc1o.ap().opt()])
                S["cc1d"].inc(cci, 1)
                G.wait_ge(S["cc1d"].h, 1)
                e_gthr = S["gfet"].inc(G.dma_start(out=gthr[:, :], in_=cc1o[:, :]), 16)

            e_ln2 = S["dve"].n
            if ATT_LN2:
                e_rstd = emit_ln_sweep(e_ffn1)
                prev_cast = [0, 0]
                for t in range(NT):
                    S["dve"].wait(V, e_rstd)
                    e_ytm = S["dve"].inc(V.tensor_scalar(
                        ytm[:, :], xtile(t), stats[:, t:t + 1], stats[:, 16 + t:17 + t],
                        ALU.subtract, ALU.mult))
                    S["dve"].wait(T, e_ytm)
                    if t == 0:
                        S["pe"].wait(T, e_ffn1_pe)
                    for kc in range(8):
                        buf = kc % 2
                        S["dve"].wait(T, prev_cast[buf])
                        e_tr = S["pe"].inc(T.transpose(pb[4 + buf][:, 0:128],
                                                       ytm[:, 128 * kc:128 * (kc + 1)],
                                                       identf[:, :]))
                        S["pe"].wait(V, e_tr)
                        prev_cast[buf] = S["dve"].inc(V.tensor_copy(
                            ych(kc)[:, 2 + 128 * t:2 + 128 * (t + 1)],
                            pb[4 + buf][:, 0:128]))
                e_ln2 = S["dve"].n

            e_y2full = S["dve"].n
            if ATT_EXTRAS:
                S["gfet"].wait(T, e_gthr)
                e_x1ex = 0
                for half in range(2):
                    if half == 0:
                        S["dve"].wait(T, e_ln2)
                    else:
                        S["dve"].wait(T, e_x1ex)
                    e_mm = S["pe"].inc(T.matmul(pb[6][0:5, :], selm_sb[:, :],
                                                gthr[:, 512 * half:512 * (half + 1)],
                                                start=True, stop=True))
                    S["pe"].wait(V, e_mm)
                    e_x1ex = S["dve"].inc(V.tensor_copy(
                        x1ex[0:5, 512 * half:512 * (half + 1)], pb[6][0:5, :]))
                S["dve"].wait(V, e_x1ex)
                V.bn_stats(stats[0:5, 64:70], x1ex[0:5, 0:512])
                e = S["dve"].inc(V.bn_stats(stats[0:5, 70:76], x1ex[0:5, 512:1024]))
                S["dve"].wait(V, e)
                e = S["dve"].inc(V.bn_aggr(stats[0:5, 76:78], stats[0:5, 64:76]))
                S["dve"].wait(V, e)
                e = S["dve"].inc(V.tensor_scalar_add(stats[0:5, 77:78],
                                                     stats[0:5, 77:78], 1e-5))
                S["dve"].wait(C, e)
                e = S["act"].inc(C.activation(stats[0:5, 78:79], stats[0:5, 77:78],
                                              AF.Sqrt))
                S["act"].wait(V, e)
                e = S["dve"].inc(V.reciprocal(stats[0:5, 79:80], stats[0:5, 78:79]))
                S["dve"].wait(V, e)
                e_yex = S["dve"].inc(V.tensor_scalar(
                    ytm[0:5, :], x1ex[0:5, :], stats[0:5, 76:77], stats[0:5, 79:80],
                    ALU.subtract, ALU.mult))
                S["dve"].wait(T, e_yex)
                e_y2full = 0
                for kc in range(8):
                    if kc > 0:
                        S["dve"].wait(T, e_y2full)
                    e_tr = S["pe"].inc(T.transpose(pb[4][:, 0:5],
                                                   ytm[0:5, 128 * kc:128 * (kc + 1)],
                                                   identf[0:5, 0:5]))
                    S["pe"].wait(V, e_tr)
                    V.tensor_copy(ych(kc)[:, 0:2], pb[4][:, 0:2])
                    V.tensor_copy(ych(kc)[:, 2050:2052], pb[4][:, 2:4])
                    e_y2full = S["dve"].inc(V.tensor_copy(ych(kc)[:, 2052:2053],
                                                          pb[4][:, 4:5]))

            S["pe"].wait(V, S["pe"].n)
            e_ones0 = S["dve"].inc(V.memset(ones0q[0:1, :], 1.0))

            def load_wT(dram, slotfn, wait_pe, sem):
                S["pe"].wait(G, wait_pe)
                ev = 0
                for kc in range(8):
                    ev = S[sem].inc(G.dma_start(
                        out=slotfn(kc), in_=dram[128 * kc:128 * (kc + 1), :]), 16)
                return ev

            ev_wk = load_wT(wT_d["wkT"], wslot1, e_ffn1_pe, "awk")

            e_extr_pe = S["pe"].n
            ev_wo = load_wT(wT_d["woT"], wslot2, e_extr_pe, "awo")

            e_q_act = 0
            e_vones = 0
            e_q_pe = 0
            if ATT_KVQ:
                S["pe"].wait(V, e_ffn1_pe)
                e_ones32 = S["dve"].inc(V.memset(ones32[0:1, :], 1.0))
                S["awk"].wait(T, ev_wk)
                S["ldg"].wait(T, e_ldg)
                S["dve"].wait(T, e_ones32)
                kprev = [0, 0]
                for s in range(5):
                    S["dve"].wait(T, e_y2full if s in (0, 4) else e_ln2)
                    for ic in range(8):
                        bank = pb[ic % 2]
                        S["act"].wait(T, kprev[ic % 2])
                        w = 512 if s < 4 else 5
                        c0 = 512 * s if s < 4 else 2048
                        for kc in range(8):
                            T.matmul(bank[:, 0:w], wslot1(kc)[:, 128 * ic:128 * (ic + 1)],
                                     ych(kc)[:, c0:c0 + w], start=(kc == 0), stop=False)
                        e_mm = S["pe"].inc(T.matmul(
                            bank[:, 0:w], brow1[32:33, 128 * ic:128 * (ic + 1)],
                            ones32[0:1, 0:w], start=False, stop=True))
                        S["pe"].wait(C, e_mm)
                        kprev[ic % 2] = S["act"].inc(C.activation(
                            kfm(ic)[:, c0:c0 + w], bank[:, 0:w], AF.Copy, scale=0.125))
                e_k_pe = S["pe"].n

                ev_wv = load_wT(wT_d["wvT"], wslot1, e_k_pe, "awv")
                S["awv"].wait(T, ev_wv)
                vprev = [0, 0]
                e_vones = 0
                for j in range(18):
                    nrow = 128 if j < 16 else (4 if j == 16 else 1)
                    c0 = 128 * j if j < 16 else (2048 if j == 16 else 2052)
                    tgt = vtile(j) if j < 16 else (vtile(16) if j == 16 else v0row)
                    for dh in range(2):
                        bank = pb[2 + dh]
                        S["act"].wait(T, vprev[dh])
                        for kc in range(8):
                            T.matmul(bank[0:nrow, :], ych(kc)[:, c0:c0 + nrow],
                                     wslot1(kc)[:, 512 * dh:512 * (dh + 1)],
                                     start=(kc == 0), stop=False)
                        e_mm = S["pe"].inc(T.matmul(
                            bank[0:nrow, :], ones64[0:1, 0:nrow],
                            brow1[64:65, 512 * dh:512 * (dh + 1)],
                            start=False, stop=True))
                        S["pe"].wait(C, e_mm)
                        vprev[dh] = S["act"].inc(C.copy(
                            tgt[0:nrow, 512 * dh:512 * (dh + 1)], bank[0:nrow, :]))
                    e_vones = S["dve"].inc(V.memset(tgt[0:nrow, 1024:1025], 1.0))
                e_v_pe = S["pe"].n

                ev_wq = load_wT(wT_d["wqT"], wslot1, e_v_pe, "awq")
                S["awq"].wait(T, ev_wq)
                e_q0 = 0
                S["dve"].wait(T, e_ones0)
                for ic in range(8):
                    bank = pb[ic % 2]
                    S["act"].wait(T, vprev[ic % 2] if ic < 2 else e_q0)
                    for kc in range(8):
                        T.matmul(bank[:, 0:1], wslot1(kc)[:, 128 * ic:128 * (ic + 1)],
                                 ych(kc)[:, 2052:2053], start=(kc == 0), stop=False)
                    e_mm = S["pe"].inc(T.matmul(
                        bank[:, 0:1], brow1[0:1, 128 * ic:128 * (ic + 1)],
                        ones0q[0:1, 0:1], start=False, stop=True))
                    S["pe"].wait(C, e_mm)
                    e_q0 = S["act"].inc(C.copy(q0_sb[:, ic:ic + 1], bank[:, 0:1]))
                S["act"].wait(V, e_q0)
                e_q0m = S["dve"].inc(V.memset(q0m[:, :], 0.0))
                S["dve"].wait(V, e_q0m)
                for h in range(NH):
                    e_q0m = S["dve"].inc(V.tensor_copy(
                        q0m[64 * (h % 2):64 * (h % 2) + 64, h:h + 1],
                        q0_sb[64 * (h % 2):64 * (h % 2) + 64, h // 2:h // 2 + 1]))
                qprev = [0] * 8
                for s in range(4):
                    e_strip_mm = 0
                    for ic in range(8):
                        S["act"].wait(T, e_q0 if s == 0 else qprev[ic])
                        for kc in range(8):
                            T.matmul(pb[ic][:, :],
                                     wslot1(kc)[:, 128 * ic:128 * (ic + 1)],
                                     ych(kc)[:, 2 + 512 * s:2 + 512 * (s + 1)],
                                     start=(kc == 0), stop=False)
                        e_strip_mm = S["pe"].inc(T.matmul(
                            pb[ic][:, :], brow1[0:1, 128 * ic:128 * (ic + 1)],
                            ones0q[0:1, 0:512], start=False, stop=True))
                    for ic in range(8):
                        S["pe"].wait(C, e_strip_mm)
                        qprev[ic] = S["act"].inc(C.copy(
                            ych(ic)[:, 512 * s:512 * (s + 1)], pb[ic][:, :]))
                e_q_act = S["act"].n

            def qsl(h, j):
                return ych(h // 2)[64 * (h % 2):64 * (h % 2) + 64,
                                   128 * j:128 * (j + 1)]

            def ksl(h, c0, w):
                return kfm(h // 2)[64 * (h % 2):64 * (h % 2) + 64, c0:c0 + w]

            ev_score, ev_gate, ev_exp, ev_pn, ev_pnwar = {}, {}, {}, {}, {}
            ev_pg = {}
            ev_ctx, ev_ctxev, ev_pnTev = {}, {}, {}

            SBK = [0, 1, 4, 5]
            def emit_scores(j, h):
                bank = pb[SBK[h % 4]]
                hh = (j, h - 4) if h >= 4 else (j - 1, h + 12)
                if hh in ev_gate:
                    S["dve"].wait(T, ev_gate[hh])
                    S["act"].wait(T, ev_exp[hh])
                if j == 0 and h in (0, 1):
                    S["act"].wait(T, e_q_act)
                T.matmul(bank[:, 0:132], qsl(h, j), ksl(h, 128 * j, 132),
                         start=True, stop=False)
                ev_score[(j, h)] = S["pe"].inc(
                    T.matmul(bank[:, 132:133], qsl(h, j), ksl(h, 2052, 1),
                             start=False, stop=True))

            def emit_epilogue_A(j, h):
                bank = pb[SBK[h % 4]]
                buf = h % 4
                S["pe"].wait(V, ev_score[(j, h)])
                hh = (j, h - 4) if h >= 4 else (j - 1, h + 12)
                if hh in ev_pnwar:
                    S["pe"].wait(V, ev_pnwar[hh])
                    S["dve"].wait(V, ev_pg[hh])
                    S["dve"].wait(C, ev_pn[hh])
                e = S["dve"].inc(V.tensor_add(bank[:, 0:133], bank[:, 0:133],
                                              bias133[:, 0:133]))
                S["dve"].wait(V, e)
                ev_gate[(j, h)] = S["dve"].inc(V.tensor_scalar(
                    mf_sb[:, 133 * buf:133 * (buf + 1)], bank[:, 0:133],
                    0.0, None, ALU.is_ge))
                S["dve"].wait(C, ev_gate[(j, h)])
                ev_exp[(j, h)] = S["act"].inc(C.activation(
                    p_sb[:, 133 * buf:133 * (buf + 1)], bank[:, 0:133],
                    AF.Exp, bias=negmf[:, 0:1], scale=1.0))

            def emit_epilogue_B(j, h):
                buf = h % 4
                S["act"].wait(V, ev_exp[(j, h)])
                e_pg = S["dve"].inc(V.scalar_tensor_tensor(
                    p_sb[:, 133 * buf:133 * (buf + 1)],
                    p_sb[:, 133 * buf:133 * (buf + 1)], 1.0,
                    mf_sb[:, 133 * buf:133 * (buf + 1)],
                    ALU.mult, ALU.mult,
                    accum_out=stats[:, 64 + buf:65 + buf]))
                ev_pg[(j, h)] = e_pg
                S["dve"].wait(V, e_pg)
                dcol = dead_sb[:, NH * j + h:NH * j + h + 1]
                e = S["dve"].inc(V.tensor_scalar(dcol, stats[:, 64 + buf:65 + buf],
                                                 0.0, None, ALU.is_le))
                if j == 0:
                    S["dve"].wait(V, e)
                    S["dve"].inc(V.tensor_scalar(dcol, dcol, q0dm_sb[:, 0:1],
                                                 None, ALU.mult))
                e = S["dve"].inc(V.tensor_scalar_max(stats[:, 70 + buf:71 + buf],
                                                     stats[:, 64 + buf:65 + buf],
                                                     1e-30))
                S["dve"].wait(V, e)
                e = S["dve"].inc(V.reciprocal(stats[:, 74 + buf:75 + buf],
                                              stats[:, 70 + buf:71 + buf]))
                S["dve"].wait(V, e)
                ev_pn[(j, h)] = S["dve"].inc(V.tensor_scalar_mul(
                    pn_sb[:, 133 * buf:133 * (buf + 1)],
                    p_sb[:, 133 * buf:133 * (buf + 1)],
                    stats[:, 74 + buf:75 + buf]))
                ev_pnwar[(j, h)] = 0

            def emit_ctx(j, h):
                buf = h % 4
                pT_prev = (j, h - 1) if h >= 1 else (j - 1, 15)
                if pT_prev in ev_pnTev:
                    S["act"].wait(T, ev_pnTev[pT_prev])
                S["dve"].wait(T, ev_pn[(j, h)])
                if j == 0 and h == 0:
                    S["dve"].wait(T, e_vones)
                T.transpose(pb6b[:, 0:128], pn_sb[:, 133 * buf:133 * buf + 128],
                            identb[:, :])
                T.transpose(pb6b[0:4, 128:256],
                            pn_sb[:, 133 * buf + 128:133 * buf + 132],
                            identb[:, :])
                e_tr = S["pe"].inc(T.transpose(
                    pb6b[0:1, 256:384],
                    pn_sb[:, 133 * buf + 132:133 * buf + 133], identb[:, :]))
                ev_pnwar[(j, h)] = e_tr
                S["pe"].wait(C, e_tr)
                C.copy(pnT_sb[:, 0:128], pb6b[:, 0:128])
                C.copy(pnT_sb[0:4, 128:256], pb6b[0:4, 128:256])
                ev_pnTev[(j, h)] = S["act"].inc(
                    C.copy(pnT_sb[0:1, 256:384], pb6b[0:1, 256:384]))
                cb = pb[2 + (h % 2)]
                cprev = (j, h - 2) if h >= 2 else (j - 1, h + 14)
                if cprev in ev_ctxev:
                    S["act"].wait(T, ev_ctxev[cprev])
                S["act"].wait(T, ev_pnTev[(j, h)])
                T.matmul(cb[0:64, 0:128], vtile(j)[:, 64 * h:64 * h + 64],
                         pnT_sb[:, 0:128], start=True, stop=False)
                T.matmul(cb[0:64, 0:128], vtile(j + 1)[0:4, 64 * h:64 * h + 64],
                         pnT_sb[0:4, 128:256], start=False, stop=False)
                ev_ctx[(j, h)] = S["pe"].inc(T.matmul(
                    cb[0:64, 0:128], v0row[0:1, 64 * h:64 * h + 64],
                    pnT_sb[0:1, 256:384], start=False, stop=True))
                S["pe"].wait(C, ev_ctx[(j, h)])
                ev_ctxev[(j, h)] = S["act"].inc(C.copy(
                    ctxfm[64 * (h % 2):64 * (h % 2) + 64,
                          128 * (h // 2):128 * (h // 2) + 128],
                    cb[0:64, 0:128]))

            S["act"].wait(V, e_q_act)
            e_o0ms = S["dve"].inc(V.memset(o0_sb[0:17, :], 0.0))
            e_ones0b = S["dve"].inc(V.memset(ones0b[0:1, :], 1.0))
            S["act"].wait(G, e_q_act)
            e_bo = S["gfet"].inc(G.dma_start(out=brow2[0:1, :], in_=b_d["bo"]), 16)
            e_s0exp_prev = 0
            e_o0g2_prev = 0
            e_o0g13_prev = e_o0ms
            e_oproj_prev = 0
            e_band_last_dve = 0
            ev_biaslast = 0
            for j in range(NBAND):
                bbv = 0 if j == 0 else (2 if j == 15 else 1)
                S["dve"].wait(V, ev_biaslast)
                V.tensor_copy(bias133[:, 0:132],
                              bb_sb[:, 132 * bbv:132 * (bbv + 1)])
                e_b133 = S["dve"].inc(V.tensor_copy(bias133[:, 132:133],
                                                    k0b_sb[:, j:j + 1]))
                S["dve"].wait(V, e_b133)
                for hh0 in range(4):
                    emit_scores(j, hh0)
                emit_epilogue_A(j, 0)
                emit_epilogue_A(j, 1)
                for h in range(NH):
                    if h + 2 < NH:
                        emit_epilogue_A(j, h + 2)
                    emit_epilogue_B(j, h)
                    if h + 4 < NH:
                        emit_scores(j, h + 4)
                    emit_ctx(j, h)
                    if h == NH - 1:
                        ev_biaslast = ev_gate[(j, h)]
                if not BAND_OPROJ:
                    continue
                S["awo"].wait(T, ev_wo)
                S["gfet"].wait(T, e_bo)
                S["dve"].wait(T, e_ones0b)
                for dh in range(2):
                    S["dve"].wait(T, e_oproj_prev)
                    S["act"].wait(T, ev_ctxev[(j, 15)])
                    for ic in range(8):
                        T.matmul(pb[7][:, :], ctxfm[:, 128 * ic:128 * (ic + 1)],
                                 wslot2(ic)[:, 512 * dh:512 * (dh + 1)],
                                 start=(ic == 0), stop=False)
                    e_mm = S["pe"].inc(T.matmul(
                        pb[7][:, :], ones0b[0:1, 0:128],
                        brow2[0:1, 512 * dh:512 * (dh + 1)],
                        start=False, stop=True))
                    S["pe"].wait(V, e_mm)
                    e_oproj_prev = S["dve"].inc(V.tensor_add(
                        xtile(j)[:, 512 * dh:512 * (dh + 1)],
                        xtile(j)[:, 512 * dh:512 * (dh + 1)], pb[7][:, :]))
                if not BAND_S0:
                    continue
                if S0LVL >= 2:
                    S["act"].wait(T, e_s0exp_prev)
                else:
                    S["dve"].wait(T, e_s0exp_prev)
                S["dve"].wait(T, e_o0g2_prev)
                if j == 0:
                    S["dve"].wait(T, e_q0m)
                S["dve"].wait(T, e_oproj_prev)
                for ic in range(8):
                    last = T.matmul(pb[7][:, 2 * ic:2 * ic + 2],
                                    kfm(ic)[:, 128 * j:128 * (j + 1)],
                                    q0m[:, 2 * ic:2 * ic + 2],
                                    start=(ic == 0), stop=(ic == 7),
                                    skip_group_check=True)
                e_s0 = S["pe"].inc(last)
                s0v = pb[7][:, 0:16]
                S["pe"].wait(V, e_s0)
                if S0LVL == 0:
                    e_mf0 = S["dve"].inc(V.tensor_copy(p0t[:, 0:16], s0v))
                    e_s0exp_prev = e_mf0
                    e_vmc = e_mf0
                    continue
                e = S["dve"].inc(V.tensor_add(
                    s0v, s0v,
                    r0b_sb[:, j:j + 1].broadcast_to([128, 16])))
                S["dve"].wait(V, e)
                e_mf0 = S["dve"].inc(V.tensor_scalar(p0t[:, 0:16], s0v,
                                                     0.0, None, ALU.is_ge))
                e_s0exp_prev = e_mf0
                e_vmc = e_mf0
                if S0LVL >= 2:
                    S["dve"].wait(C, e_mf0)
                    e_s0exp_prev = S["act"].inc(C.activation(
                        p0t[:, 16:32], s0v, AF.Exp, bias=negmf[:, 0:1], scale=1.0))
                if S0LVL >= 3:
                    S["act"].wait(V, e_s0exp_prev)
                    e_p0g = S["dve"].inc(V.scalar_tensor_tensor(
                        p0e[:, 1:17], p0t[:, 16:32], 1.0, p0t[:, 0:16],
                        ALU.mult, ALU.mult))
                    e_vmc = S["dve"].inc(V.tensor_copy(p0e[:, 0:1], vm_sb[:, j:j + 1]))
                if not BAND_O0:
                    continue
                S["dve"].wait(T, e_vmc)
                S["dve"].wait(T, e_o0g13_prev)
                S["act"].wait(T, ev_ctxev[(j, 14)])
                e1 = S["pe"].inc(T.matmul(pb[2][0:17, :], p0e[:, :],
                                          vtile(j)[:, 0:512],
                                          start=True, stop=True))
                S["pe"].wait(V, e1)
                e_add1 = S["dve"].inc(V.tensor_add(o0_sb[0:17, 0:512],
                                                   o0_sb[0:17, 0:512],
                                                   pb[2][0:17, :]))
                S["act"].wait(T, ev_ctxev[(j, 15)])
                e2 = S["pe"].inc(T.matmul(pb[3][0:17, :], p0e[:, :],
                                          vtile(j)[:, 512:1024],
                                          start=True, stop=True))
                S["pe"].wait(V, e2)
                e_o0g2_prev = S["dve"].inc(V.tensor_add(o0_sb[0:17, 512:1024],
                                                        o0_sb[0:17, 512:1024],
                                                        pb[3][0:17, :]))
                S["dve"].wait(T, e_add1)
                e3 = S["pe"].inc(T.matmul(pb[2][0:17, 0:1], p0e[:, :],
                                          vtile(j)[:, 1024:1025],
                                          start=True, stop=True))
                S["pe"].wait(V, e3)
                e_o0g13_prev = S["dve"].inc(V.tensor_add(o0_sb[0:17, 1024:1025],
                                                         o0_sb[0:17, 1024:1025],
                                                         pb[2][0:17, 0:1]))
            e_band_dve = S["dve"].n
            e_band_pe = S["pe"].n
            e_final_dve = e_band_dve

        if STAGE >= 3:
            S["dve"].wait(Y, e_band_dve)
            e_cc2i = S["st"].inc(Y.dma_start(out=cc2i[:, :], in_=o0_sb[:, :]), 16)
            S["st"].wait(G, e_cc2i)
            cci2 = G.collective_compute(
                "AllReduce", ALU.add,
                replica_groups=[[0, 1, 2, 3], [4, 5, 6, 7]],
                ins=[cc2i.ap().opt()], outs=[cc2o.ap().opt()])
            S["cc2d"].inc(cci2, 1)
            G.wait_ge(S["cc2d"].h, 1)
            e_cc2o = S["gfet"].inc(G.dma_start(out=o0_sb[:, :], in_=cc2o[:, :]), 16)

            S["gfet"].wait(V, e_cc2o)
            e = S["dve"].inc(V.tensor_scalar_max(l17[0:17, 0:1],
                                                 o0_sb[0:17, 1024:1025], 1e-30))
            S["dve"].wait(V, e)
            e = S["dve"].inc(V.reciprocal(l17[0:17, 1:2], l17[0:17, 0:1]))
            e_d17 = S["dve"].inc(V.tensor_scalar(l17[0:17, 2:3],
                                                 o0_sb[0:17, 1024:1025],
                                                 0.0, None, ALU.is_le))
            S["dve"].wait(V, e_d17)
            e_o0n = S["dve"].inc(V.tensor_scalar_mul(o0_sb[0:17, :],
                                                     o0_sb[0:17, :],
                                                     l17[0:17, 1:2]))
            S["dve"].wait(V, e_o0n)
            S["pe"].wait(V, e_band_pe)
            e_mv = S["dve"].inc(V.tensor_copy(ctxfm[0:1, 0:1024], o0_sb[0:1, 0:1024]))
            S["dve"].wait(T, e_d17)
            S["dve"].wait(T, e_oproj_prev)
            e_tr = S["pe"].inc(T.transpose(pb[7][0:1, 0:17], l17[0:17, 2:3],
                                           identf[0:17, 0:17]))
            S["pe"].wait(C, e_tr)
            e_dr = S["act"].inc(C.copy(deadr[0:1, 0:17], pb[7][0:1, 0:17]))
            e_prev_add = e_mv
            for half in range(2):
                S["act"].wait(T, e_dr)
                S["dve"].wait(T, e_prev_add)
                e_mm = S["pe"].inc(T.matmul(
                    pb[7][0:17, :], deadr[0:1, 0:17],
                    ctxfm[0:1, 512 * half:512 * (half + 1)],
                    start=True, stop=True))
                S["pe"].wait(V, e_mm)
                e_prev_add = S["dve"].inc(V.tensor_add(
                    o0_sb[0:17, 512 * half:512 * (half + 1)],
                    o0_sb[0:17, 512 * half:512 * (half + 1)], pb[7][0:17, :]))
            e_o0f = e_prev_add

            e_wovm_mm = 0
            e_pb4_ev = 0
            e_cx = 0
            e_mvs = 0
            for ic in range(8):
                S["dve"].wait(T, e_o0f)
                S["act"].wait(T, e_pb4_ev)
                S["dve"].wait(T, e_mvs)
                e_tr = S["pe"].inc(T.transpose(
                    pb[4][:, 0:17], o0_sb[0:17, 128 * ic:128 * (ic + 1)],
                    identf[0:17, 0:17]))
                S["pe"].wait(C, e_tr)
                C.copy(ctx0[0:64, ic:ic + 1], pb[4][0:64, 1 + 2 * ic:2 + 2 * ic])
                e_cx = S["act"].inc(C.copy(ctx0[64:128, ic:ic + 1],
                                           pb[4][64:128, 2 + 2 * ic:3 + 2 * ic]))
                e_pb4_ev = e_cx
                S["pe"].wait(V, e_tr)
                S["pe"].wait(V, e_wovm_mm)
                e_ms = S["dve"].inc(V.memset(mvsel[:, :], 0.0))
                S["dve"].wait(V, e_ms)
                V.tensor_copy(mvsel[0:64, 2 * ic:2 * ic + 1], pb[4][0:64, 0:1])
                e_mvs = S["dve"].inc(V.tensor_copy(
                    mvsel[64:128, 2 * ic + 1:2 * ic + 2], pb[4][64:128, 0:1]))
                S["dve"].wait(T, e_mvs)
                T.matmul(pb[2][0:16, :], mvsel[:, :], wslot2(ic)[:, 0:512],
                         start=(ic == 0), stop=(ic == 7))
                e_wovm_mm = S["pe"].inc(T.matmul(
                    pb[3][0:16, :], mvsel[:, :], wslot2(ic)[:, 512:1024],
                    start=(ic == 0), stop=(ic == 7)))
            e_prev = e_o0f
            for dh in range(2):
                S["act"].wait(T, e_cx)
                S["dve"].wait(T, e_prev)
                for ic in range(8):
                    ins = T.matmul(pb[7][0:1, :], ctx0[:, ic:ic + 1],
                                   wslot2(ic)[:, 512 * dh:512 * (dh + 1)],
                                   start=(ic == 0), stop=(ic == 7))
                e_mm = S["pe"].inc(ins)
                S["pe"].wait(V, e_mm)
                e_prev = S["dve"].inc(V.scalar_tensor_tensor(
                    x_sb[0:1, 512 * dh:512 * (dh + 1)], pb[7][0:1, :],
                    ownm_sb[0:1, 0:1], x_sb[0:1, 512 * dh:512 * (dh + 1)],
                    ALU.mult, ALU.add))
            e_rowfix = e_prev

            S["pe"].wait(C, S["pe"].n)
            C.copy(wovm[:, 0:512], pb[2][0:16, :])
            e_wovm = S["act"].inc(C.copy(wovm[:, 512:1024], pb[3][0:16, :]))

            e_dt = 0
            e_fixadd = e_rowfix
            for t in range(NT):
                S["act"].wait(T, e_dt)
                S["dve"].wait(T, e_band_dve)
                e_tr = S["pe"].inc(T.transpose(
                    pb6b[0:16, 0:128], dead_sb[:, NH * t:NH * (t + 1)],
                    identb[:, :]))
                S["pe"].wait(C, e_tr)
                e_dt = S["act"].inc(C.copy(deadt[0:16, :], pb6b[0:16, 0:128]))
                for dh in range(2):
                    S["act"].wait(T, e_dt)
                    S["act"].wait(T, e_wovm)
                    S["dve"].wait(T, e_fixadd)
                    e_mm = S["pe"].inc(T.matmul(
                        pb[5][:, :], deadt[0:16, :],
                        wovm[:, 512 * dh:512 * (dh + 1)],
                        start=True, stop=True))
                    S["pe"].wait(V, e_mm)
                    e_fixadd = S["dve"].inc(V.tensor_add(
                        xtile(t)[:, 512 * dh:512 * (dh + 1)],
                        xtile(t)[:, 512 * dh:512 * (dh + 1)], pb[5][:, :]))
            e_combine_dve = e_fixadd
            e_combine_pe = S["pe"].n

            e_ffn2, _ = emit_ffn(False, e_combine_dve, e_band_pe, e_combine_pe)
            e_final_dve = e_ffn2

        S["dve"].wait(Y, e_final_dve)
        xo = xout.rearrange("(t p) d -> p t d", p=128)
        for i in range(4):
            S["st"].inc(Y.dma_start(out=xo[:, 4 * i:4 * i + 4, :],
                                    in_=xv[:, 4 * i:4 * i + 4, :]), 16)
        Y.wait_ge(S["st"].h, S["st"].n)

        for nm, px in (("tensor", T), ("vector", V), ("scalar", C),
                       ("gpsimd", G), ("sync", Y)):
            getattr(block, nm)(px.flush)

    return nc


_CACHE = {}


def kernel(L0_attn_x, L0_attn_wq, L0_attn_bq, L0_attn_wk, L0_attn_bk,
           L0_attn_wv, L0_attn_bv, L0_attn_wo, L0_attn_bo,
           L0_ffn_router_kernel_DE, L0_ffn_kernel_gating_EDF,
           L0_ffn_kernel_up_EDF, L0_ffn_kernel_down_EFD):
    x = np.ascontiguousarray(np.asarray(L0_attn_x, dtype=np.float32))
    shared = dict(
        wqT=np.ascontiguousarray(np.asarray(L0_attn_wq, np.float32).T),
        wkT=np.ascontiguousarray(np.asarray(L0_attn_wk, np.float32).T),
        wvT=np.ascontiguousarray(np.asarray(L0_attn_wv, np.float32).T),
        woT=np.ascontiguousarray(np.asarray(L0_attn_wo, np.float32).T),
        bq=np.asarray(L0_attn_bq, np.float32).reshape(1, DM),
        bk=np.asarray(L0_attn_bk, np.float32).reshape(1, DM),
        bv=np.asarray(L0_attn_bv, np.float32).reshape(1, DM),
        bo=np.asarray(L0_attn_bo, np.float32).reshape(1, DM),
        router=np.ascontiguousarray(np.asarray(L0_ffn_router_kernel_DE, np.float32)),
        wg=np.ascontiguousarray(np.asarray(L0_ffn_kernel_gating_EDF, np.float32)),
        wu=np.ascontiguousarray(np.asarray(L0_ffn_kernel_up_EDF, np.float32)),
        wd=np.ascontiguousarray(np.asarray(L0_ffn_kernel_down_EFD, np.float32)),
        identb=np.eye(128, dtype=ml_dtypes.bfloat16),
        identf=np.eye(128, dtype=np.float32),
    )
    if "nc" not in _CACHE:
        _CACHE["nc"] = build_nc()
    nc = _CACHE["nc"]
    in_maps = []
    for core in range(8):
        b, rank = divmod(core, 4)
        m = dict(shared)
        m["xin"] = np.ascontiguousarray(x[b, rank * TPC:(rank + 1) * TPC, :])
        m.update(_host_consts(rank))
        in_maps.append(m)
    res = run_bass_kernel_spmd(nc, in_maps, core_ids=list(range(8)))
    out = np.empty((2, 8192, DM), dtype=np.float32)
    for core in range(8):
        b, rank = divmod(core, 4)
        out[b, rank * TPC:(rank + 1) * TPC, :] = res.results[core]["xout"]
    return out


if __name__ == "__main__":
    nc = build_nc()
    print("build ok")


# version 07 (final)
# speedup vs baseline: 1.0716x; 1.0144x over previous
import numpy as np
import ml_dtypes
from contextlib import ExitStack

import concourse.bass as bass
import concourse.mybir as mybir
from concourse.bass_utils import run_bass_kernel_spmd

F32 = mybir.dt.float32
BF16 = mybir.dt.bfloat16
AF = mybir.ActivationFunctionType
ALU = mybir.AluOpType
AX = mybir.AxisListType

NT = 16
TPC = 2048
DM = 1024
NH = 16
NE = 8
DF = 2048
MASKV = -3.0e9
MFIX = 24.0
YW = 2053

STAGE = 3
ATT_GATHER = True
ATT_LN2 = True
ATT_EXTRAS = True
ATT_KVQ = True
NBAND = 16
BAND_OPROJ = True
BAND_S0 = True
BAND_O0 = True
S0LVL = 3
PHASE2 = True
NEXP = 8


def _pos_bias(d):
    d = np.asarray(d, dtype=np.float64)
    return (np.exp(-0.5 * (d / 2.0) ** 2)
            + np.cos(2.0 * np.pi * d / 401.0)).astype(np.float32)


def _host_consts(rank):
    a = rank * TPC
    pcol = np.arange(128)
    c = np.arange(132)[None, :]
    dist = pcol[:, None] - c + 2
    base = np.where(np.abs(dist) <= 2, _pos_bias(dist), MASKV).astype(np.float32)
    bb = np.stack([base.copy(), base.copy(), base.copy()])
    if rank == 0:
        bb[0][:, 0:2] = MASKV
        bb[0][:, 2] = _pos_bias(pcol)
        bb[0][0, :] = MASKV
    if rank == 3:
        bb[2][:, 130:132] = MASKV
    k0b = np.empty((128, NT), dtype=np.float32)
    for j in range(NT):
        k0b[:, j] = _pos_bias(a + 128 * j + pcol)
    if rank == 0:
        k0b[:, 0] = MASKV
    r0b = np.full((128, 17), MASKV, dtype=np.float32)
    vm = np.zeros((128, 17), dtype=np.float32)
    for j in range(17):
        kl = 128 * j + pcol - 2
        valid = (kl >= 0) & (kl < TPC)
        r0b[valid, j] = _pos_bias(a + kl[valid])
        vm[valid, j] = 1.0
    sel = np.zeros((16, 5), dtype=np.float32)
    if rank > 0:
        sel[4 * (rank - 1) + 2, 0] = 1.0
        sel[4 * (rank - 1) + 3, 1] = 1.0
    if rank < 3:
        sel[4 * (rank + 1) + 0, 2] = 1.0
        sel[4 * (rank + 1) + 1, 3] = 1.0
    sel[0, 4] = 1.0
    q0dm = np.ones((128, 1), dtype=np.float32)
    if rank == 0:
        q0dm[0, 0] = 0.0
    ownm = np.full((1, 1), 1.0 if rank == 0 else 0.0, dtype=np.float32)
    return dict(bb=bb.reshape(3 * 128, 132), k0b=k0b, r0b=r0b, vm=vm,
                selm=sel.astype(ml_dtypes.bfloat16), q0dm=q0dm, ownm=ownm)


class _Pend:
    __slots__ = ("op", "args", "kwargs", "incs")


class EngProxy:

    def __init__(self, name):
        self._name = name
        self._q = []

    def __getattr__(self, op):
        def call(*a, **k):
            h = _Pend()
            h.op, h.args, h.kwargs, h.incs = op, a, k, []
            self._q.append(h)
            return h
        return call

    def flush(self, e):
        for h in self._q:
            ins = getattr(e, h.op)(*h.args, **h.kwargs)
            for s, k in h.incs:
                ins.then_inc(s, k)


class Sem:
    __slots__ = ("h", "n")

    def __init__(self, h):
        self.h = h
        self.n = 0

    def inc(self, ins, k=1):
        ins.incs.append((self.h, k))
        self.n += k
        return self.n

    def wait(self, eng, val):
        assert val <= self.n, "wait on future semaphore value"
        if val > 0:
            eng.wait_ge(self.h, val)


def build_nc():
    nc = bass.Bass("TRN2", target_bir_lowering=False)

    def din(name, shape, dt=F32):
        return nc.dram_tensor(name, shape, dt, kind="ExternalInput").ap()

    xin = din("xin", [TPC, DM])
    wT_d = {w: din(w, [DM, DM]) for w in ("wqT", "wkT", "wvT", "woT")}
    b_d = {b: din(b, [1, DM]) for b in ("bq", "bk", "bv", "bo")}
    router_d = din("router", [DM, NE])
    wg_d = din("wg", [NE, DM, DF])
    wu_d = din("wu", [NE, DM, DF])
    wd_d = din("wd", [NE, DF, DM])
    bb_d = din("bb", [3 * 128, 132])
    k0b_d = din("k0b", [128, NT])
    r0b_d = din("r0b", [128, 17])
    vm_d = din("vm", [128, 17])
    selm_d = din("selm", [16, 5], BF16)
    q0dm_d = din("q0dm", [128, 1])
    ownm_d = din("ownm", [1, 1])
    identb_d = din("identb", [128, 128], BF16)
    identf_d = din("identf", [128, 128], F32)

    xout = nc.dram_tensor("xout", [TPC, DM], F32, kind="ExternalOutput").ap()

    wgbf = nc.dram_tensor("wgbf", [NE, DM, DF], BF16)
    wubf = nc.dram_tensor("wubf", [NE, DM, DF], BF16)
    cc1i = nc.dram_tensor("cc1i", [4, DM], F32)
    cc1o = nc.dram_tensor("cc1o", [16, DM], F32)
    cc2i = nc.dram_tensor("cc2i", [17, 1025], F32)
    cc2o = nc.dram_tensor("cc2o", [17, 1025], F32)

    with ExitStack() as ctx:
        def sbt(name, shape, dt):
            return ctx.enter_context(nc.sbuf_tensor(name, shape, dt))

        x_sb = sbt("x_sb", [128, NT * DM], F32)
        arenaA = sbt("arenaA", [128, 33856], BF16)
        y2q = sbt("y2q", [128, 8 * YW], BF16)
        arenaB = sbt("arenaB", [128, 16384], BF16)
        stats = sbt("stats", [128, 80], F32)
        fw_sb = sbt("fw_sb", [128, NT * NE], F32)
        lg_sb = sbt("lg_sb", [128, NT * NE], F32)
        m2_sb = sbt("m2_sb", [128, NT * NE], BF16)
        rt_sb = sbt("rt_sb", [128, 48], F32)
        rt_w = sbt("rt_w", [128, 8 * NE], F32)
        bb_sb = sbt("bb_sb", [128, 3 * 132], F32)
        k0b_sb = sbt("k0b_sb", [128, NT], F32)
        r0b_sb = sbt("r0b_sb", [128, 17], F32)
        vm_sb = sbt("vm_sb", [128, 17], F32)
        q0dm_sb = sbt("q0dm_sb", [128, 1], F32)
        ownm_sb = sbt("ownm_sb", [1, 1], F32)
        identb = sbt("identb_s", [128, 128], BF16)
        identf = sbt("identf_s", [128, 128], F32)
        pnT_sb0 = sbt("pnT_sb", [128, 384], BF16)
        ctxfm = sbt("ctxfm", [128, 1024], BF16)
        dead_sb = sbt("dead_sb", [128, NT * NH], BF16)
        p0e = sbt("p0e", [128, 17], BF16)
        p0t = sbt("p0t", [128, 32], F32)
        q0_sb = sbt("q0_sb", [128, 8], BF16)
        q0m = sbt("q0m", [128, 16], BF16)
        ctx0 = sbt("ctx0", [128, 8], BF16)
        mvsel = sbt("mvsel", [128, 16], BF16)
        brow1 = sbt("brow1", [65, DM], BF16)
        gthr = None
        selm_sb = sbt("selm_sb", [16, 5], BF16)
        deadr = sbt("deadr", [1, 17], BF16)
        mvrow = None
        l17 = sbt("l17", [17, 4], F32)
        v0row_t = sbt("v0row", [1, 1025], BF16)
        negmf = sbt("negmf", [128, 1], F32)

        v0row = vtile_slot16 = None
        pb = [ctx.enter_context(nc.psum_tensor(f"pb{i}", [128, 512], F32))
              for i in range(8)]
        pb6b = pb[6][:, :].bitcast(BF16)
        _stg = arenaB[:, 12288:16384].bitcast(F32)
        _wv1f = arenaB[:, 0:8192].bitcast(F32)
        sstage = _stg[:, 0:1024]
        ytm = _stg[:, 0:1024]
        yfev = _stg[:, 1536:1792]
        o0_sb = _wv1f[0:17, 0:1025]
        p_sb = _wv1f[:, 1030:1562]
        bias133 = _wv1f[:, 2880:3013]
        mf_sb = arenaB[:, 3124:3656]
        pn_sb = arenaB[:, 3656:4188]
        pnT_alt = arenaB[:, 6100:6484]

        S = {n: Sem(ctx.enter_context(nc.semaphore(n)))
             for n in ["ldx0", "ldx1", "ldx2", "ldx3", "ldc", "ldg", "wld", "wdld0", "wdld1", "wdld2",
                       "wdld3", "awk", "awv", "awq", "awo", "gfet", "wldb", "pe", "act",
                       "dve", "pool", "cc1d", "cc2d", "st"] +
                      [f"cvt{i}" for i in range(8)]}

        block = ctx.enter_context(nc.Block())
        T, V, C, G, Y = (EngProxy("tensor"), EngProxy("vector"),
                         EngProxy("scalar"), EngProxy("gpsimd"),
                         EngProxy("sync"))

        def wgc(kc):
            return arenaA[:, 2048 * kc:2048 * (kc + 1)]

        def wuc(kc):
            return arenaA[:, 16384 + 2048 * kc:16384 + 2048 * (kc + 1)]

        def kfm(ic):
            return arenaA[:, YW * ic:YW * (ic + 1)]

        def vtile(j):
            return arenaA[:, 16424 + 1025 * j:16424 + 1025 * (j + 1)]

        def ych(kc):
            return y2q[:, YW * kc:YW * (kc + 1)]

        def fch(fc):
            return arenaB[:, 512 * fc:512 * (fc + 1)]

        def wdch(slot):
            return arenaB[:, 8192 + 1024 * slot:8192 + 1024 * (slot + 1)]

        def wslot1(kc):
            return arenaB[:, 1024 * kc:1024 * (kc + 1)]

        def wslot2(kc):
            return arenaB[:, 8192 + 1024 * kc:8192 + 1024 * (kc + 1)]

        def xtile(t):
            return x_sb[:, DM * t:DM * (t + 1)]

        ones64 = vtile(16)[64:65, 0:512]
        v0row = v0row_t[0:1, :]
        ones32 = ctxfm[32:33, 0:512]
        deadt = pn_sb[0:16, 0:128]
        ones0q = ctxfm[0:1, 512:1024]
        ones0b = arenaB[0:1, 4224:4736]
        brow2 = arenaB[0:1, 4736:5760]
        gthr = ctxfm[0:16, :]
        wovm = arenaB[0:16, 12288:13312]
        x1ex = ctxfm[0:5, :]

        xr = xin.rearrange("(t p) d -> p t d", p=128)
        xv = x_sb[:, :].rearrange("p (t d) -> p t d", d=DM)
        for i in range(4):
            S[f"ldx{i}"].inc(Y.dma_start(out=xv[:, 4 * i:4 * i + 4, :],
                                        in_=xr[:, 4 * i:4 * i + 4, :]), 16)
        e_ldx = 16
        for dst, src in [(bb_sb[:, :].rearrange("p (v c) -> p v c", v=3),
                          bb_d.rearrange("(v p) c -> p v c", p=128)),
                         (k0b_sb, k0b_d), (r0b_sb, r0b_d), (vm_sb, vm_d),
                         (q0dm_sb, q0dm_d), (ownm_sb, ownm_d),
                         (identb, identb_d), (identf, identf_d),
                         (selm_sb, selm_d)]:
            S["ldc"].inc(Y.dma_start(
                out=dst if isinstance(dst, bass.AP) else dst[:, :],
                in_=src), 16)
        for kc in range(8):
            S["ldc"].inc(Y.dma_start(out=rt_w[:, 8 * kc:8 * kc + 8],
                                     in_=router_d[128 * kc:128 * kc + 128, :]), 16)
        S["ldg"].inc(G.dma_start(out=brow1[0:1, :], in_=b_d["bq"]), 16)
        S["ldg"].inc(G.dma_start(out=brow1[32:33, :], in_=b_d["bk"]), 16)
        S["ldg"].inc(G.dma_start(out=brow1[64:65, :], in_=b_d["bv"]), 16)
        e_ldc = S["ldc"].n
        e_ldg = S["ldg"].n
        V.memset(ones64[:, :], 1.0)
        V.memset(negmf[:, :], -MFIX)

        wd_state = {"seq": 0, "evdma": {}, "evfree": {}}
        conv_order = [1, 2, 3, 4, 5, 6, 7, 0]
        cvt_todo = [(e, w, kc) for e in conv_order for w in (0, 1)
                    for kc in range(8)]

        def emit_ln_sweep(e_x_dve, first=False):
            S["dve"].wait(V, e_x_dve)
            ev = 0
            for t in range(NT):
                if first and t % 4 == 0:
                    S[f"ldx{t // 4}"].wait(V, 16)
                V.bn_stats(stats[:, 64:70], xtile(t)[:, 0:512])
                e = S["dve"].inc(V.bn_stats(stats[:, 70:76], xtile(t)[:, 512:1024]))
                S["dve"].wait(V, e)
                e = S["dve"].inc(V.bn_aggr(stats[:, 76:78], stats[:, 64:76]))
                S["dve"].wait(V, e)
                V.tensor_copy(stats[:, t:t + 1], stats[:, 76:77])
                ev = S["dve"].inc(V.tensor_scalar_add(stats[:, 32 + t:33 + t],
                                                      stats[:, 77:78], 1e-5))
            S["dve"].wait(C, ev)
            e_sig = S["act"].inc(C.activation(stats[:, 48:64], stats[:, 32:48],
                                              AF.Sqrt))
            S["act"].wait(V, e_sig)
            return S["dve"].inc(V.reciprocal(stats[:, 16:32], stats[:, 48:64]))

        def emit_ffn(first_call, e_x_dve, e_arenaA_pe, e_arenaB_pe):
            e_rstd = emit_ln_sweep(e_x_dve, first=first_call)

            ev_lg = {}
            prev_cast = [0, 0]
            prev_mm = [0, 0]
            S["pe"].wait(V, e_arenaB_pe)
            for t in range(NT):
                S["dve"].wait(V, e_rstd)
                e_ytm = S["dve"].inc(V.tensor_scalar(
                    ytm[:, :], xtile(t), stats[:, t:t + 1], stats[:, 16 + t:17 + t],
                    ALU.subtract, ALU.mult))
                S["dve"].wait(T, e_ytm)
                if t == 0:
                    S["ldc"].wait(T, e_ldc)
                for kc in range(8):
                    buf = kc % 2
                    S["dve"].wait(T, prev_cast[buf])
                    S["pe"].wait(T, prev_mm[buf])
                    e_tr = S["pe"].inc(T.transpose(pb[4 + buf][:, 0:128],
                                                   ytm[:, 128 * kc:128 * (kc + 1)],
                                                   identf[:, :]))
                    S["pe"].wait(C, e_tr)
                    e_ev = S["act"].inc(C.copy(yfev[:, 128 * buf:128 * (buf + 1)],
                                               pb[4 + buf][:, 0:128]))
                    S["act"].wait(V, e_ev)
                    prev_cast[buf] = S["dve"].inc(V.tensor_copy(
                        ych(kc)[:, 128 * t:128 * (t + 1)],
                        yfev[:, 128 * buf:128 * (buf + 1)]))
                    S["act"].wait(T, e_ev)
                    prev_mm[buf] = S["pe"].inc(T.matmul(
                        pb[6][:, 0:8], yfev[:, 128 * buf:128 * (buf + 1)],
                        rt_w[:, 8 * kc:8 * kc + 8],
                        start=(kc == 0), stop=(kc == 7)))
                S["pe"].wait(V, prev_mm[1])
                ev_lg[t] = S["dve"].inc(V.tensor_copy(lg_sb[:, 8 * t:8 * t + 8],
                                                      pb[6][:, 0:8]))
            e_ycast = S["dve"].n

            e_stash = 0
            for t in range(NT):
                L = lg_sb[:, 8 * t:8 * t + 8]
                S["dve"].wait(V, ev_lg[t])
                e = S["dve"].inc(V.reduce_max(rt_sb[:, 0:1], L, axis=AX.X))
                S["dve"].wait(V, e)
                e = S["dve"].inc(V.tensor_scalar(rt_sb[:, 2:10], L, rt_sb[:, 0:1],
                                                 None, ALU.is_ge))
                S["dve"].wait(V, e)
                e = S["dve"].inc(V.scalar_tensor_tensor(
                    rt_sb[:, 10:18], rt_sb[:, 2:10], -1.0e30, L, ALU.mult, ALU.add))
                S["dve"].wait(V, e)
                e = S["dve"].inc(V.reduce_max(rt_sb[:, 1:2], rt_sb[:, 10:18],
                                              axis=AX.X))
                S["dve"].wait(V, e)
                e = S["dve"].inc(V.tensor_scalar(rt_sb[:, 18:26], rt_sb[:, 10:18],
                                                 rt_sb[:, 1:2], None, ALU.is_ge))
                ng = S["dve"].inc(V.tensor_scalar_mul(rt_sb[:, 40:41],
                                                      rt_sb[:, 0:1], -1.0))
                S["dve"].wait(C, ng)
                e_exp = S["act"].inc(C.activation(rt_sb[:, 26:34], L, AF.Exp,
                                                  bias=rt_sb[:, 40:41], scale=1.0,
                                                  accum_out=rt_sb[:, 34:35]))
                S["act"].wait(V, e_exp)
                e = S["dve"].inc(V.tensor_mul(rt_sb[:, 10:18], rt_sb[:, 26:34],
                                              rt_sb[:, 2:10]))
                S["dve"].wait(V, e)
                e = S["dve"].inc(V.reduce_sum(rt_sb[:, 35:36], rt_sb[:, 10:18],
                                              axis=AX.X))
                S["dve"].wait(V, e)
                e = S["dve"].inc(V.tensor_mul(rt_sb[:, 10:18], rt_sb[:, 26:34],
                                              rt_sb[:, 18:26]))
                S["dve"].wait(V, e)
                e = S["dve"].inc(V.reduce_sum(rt_sb[:, 36:37], rt_sb[:, 10:18],
                                              axis=AX.X))
                S["dve"].wait(V, e)
                S["dve"].inc(V.tensor_sub(rt_sb[:, 37:38], rt_sb[:, 35:36],
                                          rt_sb[:, 36:37]))
                e = S["dve"].inc(V.reciprocal(rt_sb[:, 38:39], rt_sb[:, 34:35]))
                S["dve"].wait(V, e)
                e = S["dve"].inc(V.tensor_mul(rt_sb[:, 39:40], rt_sb[:, 37:38],
                                              rt_sb[:, 38:39]))
                S["dve"].wait(V, e)
                S["dve"].inc(V.tensor_copy(fw_sb[:, 8 * t:8 * t + 1],
                                           rt_sb[:, 39:40]))
                S["dve"].inc(V.tensor_copy(L, rt_sb[:, 2:10]))
                e_stash = S["dve"].inc(V.tensor_copy(m2_sb[:, 8 * t:8 * t + 8],
                                                     rt_sb[:, 18:26]))
            argall = fw_sb[:, :].rearrange("p (t e) -> p t e", e=NE)[:, :, 0]
            S["dve"].wait(C, e_stash)
            e_sg = S["act"].inc(C.activation(stats[:, 48:64], argall, AF.Sigmoid))
            S["act"].wait(V, e_sg)
            e_fw = 0
            for t in range(NT):
                nw1 = stats[:, 48 + t:49 + t]
                S["dve"].wait(V, e_fw if t > 0 else 0)
                S["dve"].inc(V.tensor_scalar(rt_sb[:, 2:10],
                                             lg_sb[:, 8 * t:8 * t + 8],
                                             nw1, 0.5, ALU.mult, ALU.mult))
                e = S["dve"].inc(V.tensor_scalar(rt_sb[:, 10:18],
                                                 m2_sb[:, 8 * t:8 * t + 8],
                                                 nw1, 0.5, ALU.mult, ALU.mult))
                S["dve"].wait(V, e)
                e = S["dve"].inc(V.scalar_tensor_tensor(
                    rt_sb[:, 10:18], m2_sb[:, 8 * t:8 * t + 8], 0.5,
                    rt_sb[:, 10:18], ALU.mult, ALU.subtract))
                S["dve"].wait(V, e)
                e_fw = S["dve"].inc(V.tensor_add(fw_sb[:, 8 * t:8 * t + 8],
                                                 rt_sb[:, 2:10], rt_sb[:, 10:18]))

            def load_wgu(e_idx, wait_pe):
                staged = (not first_call) or e_idx >= 1
                if not staged:
                    S["pe"].wait(G, wait_pe)
                    for kc in range(8):
                        S["wld"].inc(G.dma_start(
                            out=wgc(kc),
                            in_=wg_d[e_idx, 128 * kc:128 * (kc + 1), :]), 16)
                    for kc in range(8):
                        S["wld"].inc(G.dma_start(
                            out=wuc(kc),
                            in_=wu_d[e_idx, 128 * kc:128 * (kc + 1), :]), 16)
                    return ("wld", S["wld"].n)
                S["pe"].wait(Y, wait_pe)
                assert S[f"cvt{e_idx}"].n == 256, "conversion not fully emitted"
                S[f"cvt{e_idx}"].wait(Y, 256)
                for kc in range(8):
                    S["wldb"].inc(Y.dma_start(
                        out=wgc(kc), in_=wgbf[e_idx, 128 * kc:128 * (kc + 1), :]), 16)
                for kc in range(8):
                    S["wldb"].inc(Y.dma_start(
                        out=wuc(kc), in_=wubf[e_idx, 128 * kc:128 * (kc + 1), :]), 16)
                return ("wldb", S["wldb"].n)

            wd_plan = [(e_i, fc) for e_i in range(NEXP) for _s in range(4)
                       for fc in range(16)]
            base_seq = wd_state["seq"]

            def load_wd_next(i_local, wait_pe_extra=0):
                seq = wd_state["seq"]
                e_i, fc = wd_plan[i_local]
                if wait_pe_extra:
                    S["pe"].wait(G, wait_pe_extra)
                if seq >= 4:
                    S["pe"].wait(G, wd_state["evfree"][seq - 4])
                wd_state["evdma"][seq] = S[f"wdld{seq % 4}"].inc(G.dma_start(
                    out=wdch(seq % 4),
                    in_=wd_d[e_i, 128 * fc:128 * (fc + 1), :]), 16)
                wd_state["seq"] += 1
                if first_call and cvt_todo:
                    ce, cw, ckc = cvt_todo.pop(0)
                    dsrc = wg_d if cw == 0 else wu_d
                    ddst = wgbf if cw == 0 else wubf
                    S[f"cvt{ce}"].inc(G.dma_start(
                        out=ddst[ce, 128 * ckc:128 * (ckc + 1), :],
                        in_=dsrc[ce, 128 * ckc:128 * (ckc + 1), :]), 16)

            if NEXP > 0:
                ev_wgu = load_wgu(0, e_arenaA_pe)
                for i in range(min(4, len(wd_plan))):
                    load_wd_next(i, e_arenaB_pe if i == 0 else 0)
            n_wd_loaded = min(4, len(wd_plan))

            silu_free = [0, 0]
            fmul_free = [0, 0]
            ev_fmul_fc = {}
            ev_xupd = [0] * 8
            e_p1_last = 0
            e_phase2_read = e_arenaB_pe
            for ex in range(NEXP):
                S[ev_wgu[0]].wait(T, ev_wgu[1])
                if ex == 0:
                    S["dve"].wait(T, e_fw)
                    S["dve"].wait(T, e_ycast)
                for s in range(4):
                    for fc in range(16):
                        gb, ub = pb[fc % 2], pb[2 + fc % 2]
                        S["act"].wait(T, silu_free[fc % 2])
                        S["dve"].wait(T, fmul_free[fc % 2])
                        if fc < 2:
                            S["dve"].wait(T, ev_xupd[fc % 2])
                            S["dve"].wait(T, ev_xupd[2 + fc % 2])
                        if fc == 0:
                            S["pe"].wait(V, e_phase2_read)
                        for kc in range(8):
                            ins = T.matmul(gb[:, :],
                                           wgc(kc)[:, 128 * fc:128 * (fc + 1)],
                                           ych(kc)[:, 512 * s:512 * (s + 1)],
                                           start=(kc == 0), stop=(kc == 7))
                        eg = S["pe"].inc(ins)
                        for kc in range(8):
                            ins = T.matmul(ub[:, :],
                                           wuc(kc)[:, 128 * fc:128 * (fc + 1)],
                                           ych(kc)[:, 512 * s:512 * (s + 1)],
                                           start=(kc == 0), stop=(kc == 7))
                        eu = S["pe"].inc(ins)
                        e_p1_last = eu
                        S["pe"].wait(C, eg)
                        silu_free[fc % 2] = S["act"].inc(C.activation(
                            sstage[:, 512 * (fc % 2):512 * (fc % 2 + 1)],
                            gb[:, :], AF.Silu))
                        S["pe"].wait(V, eu)
                        S["act"].wait(V, silu_free[fc % 2])
                        fmul_free[fc % 2] = S["dve"].inc(V.tensor_mul(
                            fch(fc),
                            sstage[:, 512 * (fc % 2):512 * (fc % 2 + 1)],
                            ub[:, :]))
                        ev_fmul_fc[fc] = fmul_free[fc % 2]
                    if not PHASE2:
                        continue
                    for k in range(8):
                        S["dve"].wait(T, ev_xupd[k])
                    S["act"].wait(T, silu_free[0])
                    S["act"].wait(T, silu_free[1])
                    S["dve"].wait(T, fmul_free[0])
                    S["dve"].wait(T, fmul_free[1])
                    for fc in range(16):
                        seq = base_seq + (ex * 4 + s) * 16 + fc
                        S[f"wdld{seq % 4}"].wait(T, wd_state["evdma"][seq])
                        S["dve"].wait(T, ev_fmul_fc[fc])
                        last = None
                        for tt in range(4):
                            for dh in range(2):
                                last = T.matmul(
                                    pb[tt * 2 + dh][:, :],
                                    fch(fc)[:, 128 * tt:128 * (tt + 1)],
                                    wdch(seq % 4)[:, 512 * dh:512 * (dh + 1)],
                                    start=(fc == 0), stop=(fc == 15))
                        wd_state["evfree"][seq] = S["pe"].inc(last)
                        if n_wd_loaded < len(wd_plan):
                            load_wd_next(n_wd_loaded)
                            n_wd_loaded += 1
                    e_p2 = S["pe"].n
                    if s == 3 and ex + 1 < NEXP:
                        ev_wgu = load_wgu(ex + 1, e_p1_last)
                    S["dve"].wait(V, e_fw)
                    for tt in range(4):
                        t_glob = 4 * s + tt
                        for dh in range(2):
                            S["pe"].wait(V, e_p2)
                            ev_xupd[tt * 2 + dh] = S["dve"].inc(
                                V.scalar_tensor_tensor(
                                    xtile(t_glob)[:, 512 * dh:512 * (dh + 1)],
                                    pb[tt * 2 + dh][:, :],
                                    fw_sb[:, 8 * t_glob + ex:8 * t_glob + ex + 1],
                                    xtile(t_glob)[:, 512 * dh:512 * (dh + 1)],
                                    ALU.mult, ALU.add))
                    e_phase2_read = e_p2
            return S["dve"].n, S["pe"].n

        e_ffn1, e_ffn1_pe = emit_ffn(True, 0, 0, 0)
        e_final_dve = e_ffn1

        if STAGE >= 2:
            e_gthr = 0
            if ATT_GATHER:
                S["dve"].wait(Y, e_ffn1)
                S["st"].inc(Y.dma_start(out=cc1i[0:2, :], in_=x_sb[0:2, 0:DM]), 16)
                e_cc1i = S["st"].inc(Y.dma_start(
                    out=cc1i[2:4, :], in_=x_sb[126:128, DM * 15:DM * 16]), 16)
                S["st"].wait(G, e_cc1i)
                cci = G.collective_compute(
                    "AllGather", ALU.bypass,
                    replica_groups=[[0, 1, 2, 3], [4, 5, 6, 7]],
                    ins=[cc1i.ap().opt()], outs=[cc1o.ap().opt()])
                S["cc1d"].inc(cci, 1)
                G.wait_ge(S["cc1d"].h, 1)
                e_gthr = S["gfet"].inc(G.dma_start(out=gthr[:, :], in_=cc1o[:, :]), 16)

            e_ln2 = S["dve"].n
            if ATT_LN2:
                e_rstd = emit_ln_sweep(e_ffn1)
                prev_cast = [0, 0]
                for t in range(NT):
                    S["dve"].wait(V, e_rstd)
                    e_ytm = S["dve"].inc(V.tensor_scalar(
                        ytm[:, :], xtile(t), stats[:, t:t + 1], stats[:, 16 + t:17 + t],
                        ALU.subtract, ALU.mult))
                    S["dve"].wait(T, e_ytm)
                    if t == 0:
                        S["pe"].wait(T, e_ffn1_pe)
                    for kc in range(8):
                        buf = kc % 2
                        S["dve"].wait(T, prev_cast[buf])
                        e_tr = S["pe"].inc(T.transpose(pb[4 + buf][:, 0:128],
                                                       ytm[:, 128 * kc:128 * (kc + 1)],
                                                       identf[:, :]))
                        S["pe"].wait(V, e_tr)
                        prev_cast[buf] = S["dve"].inc(V.tensor_copy(
                            ych(kc)[:, 2 + 128 * t:2 + 128 * (t + 1)],
                            pb[4 + buf][:, 0:128]))
                e_ln2 = S["dve"].n

            e_y2full = S["dve"].n
            if ATT_EXTRAS:
                S["gfet"].wait(T, e_gthr)
                e_x1ex = 0
                for half in range(2):
                    if half == 0:
                        S["dve"].wait(T, e_ln2)
                    else:
                        S["dve"].wait(T, e_x1ex)
                    e_mm = S["pe"].inc(T.matmul(pb[6][0:5, :], selm_sb[:, :],
                                                gthr[:, 512 * half:512 * (half + 1)],
                                                start=True, stop=True))
                    S["pe"].wait(V, e_mm)
                    e_x1ex = S["dve"].inc(V.tensor_copy(
                        x1ex[0:5, 512 * half:512 * (half + 1)], pb[6][0:5, :]))
                S["dve"].wait(V, e_x1ex)
                V.bn_stats(stats[0:5, 64:70], x1ex[0:5, 0:512])
                e = S["dve"].inc(V.bn_stats(stats[0:5, 70:76], x1ex[0:5, 512:1024]))
                S["dve"].wait(V, e)
                e = S["dve"].inc(V.bn_aggr(stats[0:5, 76:78], stats[0:5, 64:76]))
                S["dve"].wait(V, e)
                e = S["dve"].inc(V.tensor_scalar_add(stats[0:5, 77:78],
                                                     stats[0:5, 77:78], 1e-5))
                S["dve"].wait(C, e)
                e = S["act"].inc(C.activation(stats[0:5, 78:79], stats[0:5, 77:78],
                                              AF.Sqrt))
                S["act"].wait(V, e)
                e = S["dve"].inc(V.reciprocal(stats[0:5, 79:80], stats[0:5, 78:79]))
                S["dve"].wait(V, e)
                e_yex = S["dve"].inc(V.tensor_scalar(
                    ytm[0:5, :], x1ex[0:5, :], stats[0:5, 76:77], stats[0:5, 79:80],
                    ALU.subtract, ALU.mult))
                S["dve"].wait(T, e_yex)
                e_y2full = 0
                for kc in range(8):
                    if kc > 0:
                        S["dve"].wait(T, e_y2full)
                    e_tr = S["pe"].inc(T.transpose(pb[4][:, 0:5],
                                                   ytm[0:5, 128 * kc:128 * (kc + 1)],
                                                   identf[0:5, 0:5]))
                    S["pe"].wait(V, e_tr)
                    V.tensor_copy(ych(kc)[:, 0:2], pb[4][:, 0:2])
                    V.tensor_copy(ych(kc)[:, 2050:2052], pb[4][:, 2:4])
                    e_y2full = S["dve"].inc(V.tensor_copy(ych(kc)[:, 2052:2053],
                                                          pb[4][:, 4:5]))

            S["pe"].wait(V, S["pe"].n)
            e_ones0 = S["dve"].inc(V.memset(ones0q[0:1, :], 1.0))

            def load_wT(dram, slotfn, wait_pe, sem):
                S["pe"].wait(G, wait_pe)
                ev = 0
                for kc in range(8):
                    ev = S[sem].inc(G.dma_start(
                        out=slotfn(kc), in_=dram[128 * kc:128 * (kc + 1), :]), 16)
                return ev

            ev_wk = load_wT(wT_d["wkT"], wslot1, e_ffn1_pe, "awk")

            e_extr_pe = S["pe"].n
            ev_wv = load_wT(wT_d["wvT"], wslot2, e_extr_pe, "awv")

            e_q_act = 0
            e_vones = 0
            e_q_pe = 0
            if ATT_KVQ:
                S["pe"].wait(V, e_ffn1_pe)
                e_ones32 = S["dve"].inc(V.memset(ones32[0:1, :], 1.0))
                S["awk"].wait(T, ev_wk)
                S["ldg"].wait(T, e_ldg)
                S["dve"].wait(T, e_ones32)
                kprev = [0, 0]
                for s in range(5):
                    S["dve"].wait(T, e_y2full if s in (0, 4) else e_ln2)
                    for ic in range(8):
                        bank = pb[ic % 2]
                        S["act"].wait(T, kprev[ic % 2])
                        w = 512 if s < 4 else 5
                        c0 = 512 * s if s < 4 else 2048
                        for kc in range(8):
                            T.matmul(bank[:, 0:w], wslot1(kc)[:, 128 * ic:128 * (ic + 1)],
                                     ych(kc)[:, c0:c0 + w], start=(kc == 0), stop=False)
                        e_mm = S["pe"].inc(T.matmul(
                            bank[:, 0:w], brow1[32:33, 128 * ic:128 * (ic + 1)],
                            ones32[0:1, 0:w], start=False, stop=True))
                        S["pe"].wait(C, e_mm)
                        kprev[ic % 2] = S["act"].inc(C.activation(
                            kfm(ic)[:, c0:c0 + w], bank[:, 0:w], AF.Copy, scale=0.125))
                e_k_pe = S["pe"].n

                ev_wq = load_wT(wT_d["wqT"], wslot1, e_k_pe, "awq")
                S["awv"].wait(T, ev_wv)
                vprev = [0, 0]
                e_vones = 0
                for j in range(18):
                    nrow = 128 if j < 16 else (4 if j == 16 else 1)
                    c0 = 128 * j if j < 16 else (2048 if j == 16 else 2052)
                    tgt = vtile(j) if j < 16 else (vtile(16) if j == 16 else v0row)
                    for dh in range(2):
                        bank = pb[2 + dh]
                        S["act"].wait(T, vprev[dh])
                        for kc in range(8):
                            T.matmul(bank[0:nrow, :], ych(kc)[:, c0:c0 + nrow],
                                     wslot2(kc)[:, 512 * dh:512 * (dh + 1)],
                                     start=(kc == 0), stop=False)
                        e_mm = S["pe"].inc(T.matmul(
                            bank[0:nrow, :], ones64[0:1, 0:nrow],
                            brow1[64:65, 512 * dh:512 * (dh + 1)],
                            start=False, stop=True))
                        S["pe"].wait(C, e_mm)
                        vprev[dh] = S["act"].inc(C.copy(
                            tgt[0:nrow, 512 * dh:512 * (dh + 1)], bank[0:nrow, :]))
                    e_vones = S["dve"].inc(V.memset(tgt[0:nrow, 1024:1025], 1.0))
                e_v_pe = S["pe"].n

                ev_wo = load_wT(wT_d["woT"], wslot2, e_v_pe, "awo")
                S["awq"].wait(T, ev_wq)
                e_q0 = 0
                S["dve"].wait(T, e_ones0)
                for ic in range(8):
                    bank = pb[ic % 2]
                    S["act"].wait(T, vprev[ic % 2] if ic < 2 else e_q0)
                    for kc in range(8):
                        T.matmul(bank[:, 0:1], wslot1(kc)[:, 128 * ic:128 * (ic + 1)],
                                 ych(kc)[:, 2052:2053], start=(kc == 0), stop=False)
                    e_mm = S["pe"].inc(T.matmul(
                        bank[:, 0:1], brow1[0:1, 128 * ic:128 * (ic + 1)],
                        ones0q[0:1, 0:1], start=False, stop=True))
                    S["pe"].wait(C, e_mm)
                    e_q0 = S["act"].inc(C.copy(q0_sb[:, ic:ic + 1], bank[:, 0:1]))
                S["act"].wait(V, e_q0)
                e_q0m = S["dve"].inc(V.memset(q0m[:, :], 0.0))
                S["dve"].wait(V, e_q0m)
                for h in range(NH):
                    e_q0m = S["dve"].inc(V.tensor_copy(
                        q0m[64 * (h % 2):64 * (h % 2) + 64, h:h + 1],
                        q0_sb[64 * (h % 2):64 * (h % 2) + 64, h // 2:h // 2 + 1]))
                qprev = [0] * 8
                for s in range(4):
                    e_strip_mm = 0
                    for ic in range(8):
                        S["act"].wait(T, e_q0 if s == 0 else qprev[ic])
                        for kc in range(8):
                            T.matmul(pb[ic][:, :],
                                     wslot1(kc)[:, 128 * ic:128 * (ic + 1)],
                                     ych(kc)[:, 2 + 512 * s:2 + 512 * (s + 1)],
                                     start=(kc == 0), stop=False)
                        e_strip_mm = S["pe"].inc(T.matmul(
                            pb[ic][:, :], brow1[0:1, 128 * ic:128 * (ic + 1)],
                            ones0q[0:1, 0:512], start=False, stop=True))
                    for ic in range(8):
                        S["pe"].wait(C, e_strip_mm)
                        qprev[ic] = S["act"].inc(C.copy(
                            ych(ic)[:, 512 * s:512 * (s + 1)], pb[ic][:, :]))
                e_q_act = S["act"].n

            def qsl(h, j):
                return ych(h // 2)[64 * (h % 2):64 * (h % 2) + 64,
                                   128 * j:128 * (j + 1)]

            def ksl(h, c0, w):
                return kfm(h // 2)[64 * (h % 2):64 * (h % 2) + 64, c0:c0 + w]

            ev_score, ev_gate, ev_exp, ev_pn, ev_pnwar = {}, {}, {}, {}, {}
            ev_pg = {}
            ev_ctx, ev_ctxev, ev_pnTev = {}, {}, {}

            SBK = [0, 1, 4, 5]
            def emit_scores(j, h):
                bank = pb[SBK[h % 4]]
                hh = (j, h - 4) if h >= 4 else (j - 1, h + 12)
                if hh in ev_gate:
                    S["dve"].wait(T, ev_gate[hh])
                    S["act"].wait(T, ev_exp[hh])
                if j == 0 and h in (0, 1):
                    S["act"].wait(T, e_q_act)
                T.matmul(bank[:, 0:132], qsl(h, j), ksl(h, 128 * j, 132),
                         start=True, stop=False)
                ev_score[(j, h)] = S["pe"].inc(
                    T.matmul(bank[:, 132:133], qsl(h, j), ksl(h, 2052, 1),
                             start=False, stop=True))

            def emit_epilogue_A(j, h):
                bank = pb[SBK[h % 4]]
                buf = h % 4
                S["pe"].wait(V, ev_score[(j, h)])
                hh = (j, h - 4) if h >= 4 else (j - 1, h + 12)
                if hh in ev_pnwar:
                    S["pe"].wait(V, ev_pnwar[hh])
                    S["dve"].wait(V, ev_pg[hh])
                    S["dve"].wait(C, ev_pn[hh])
                e = S["dve"].inc(V.tensor_add(bank[:, 0:133], bank[:, 0:133],
                                              bias133[:, 0:133]))
                S["dve"].wait(V, e)
                ev_gate[(j, h)] = S["dve"].inc(V.tensor_scalar(
                    mf_sb[:, 133 * buf:133 * (buf + 1)], bank[:, 0:133],
                    0.0, None, ALU.is_ge))
                S["dve"].wait(C, ev_gate[(j, h)])
                ev_exp[(j, h)] = S["act"].inc(C.activation(
                    p_sb[:, 133 * buf:133 * (buf + 1)], bank[:, 0:133],
                    AF.Exp, bias=negmf[:, 0:1], scale=1.0))

            def emit_epilogue_B(j, h):
                buf = h % 4
                S["act"].wait(V, ev_exp[(j, h)])
                e_pg = S["dve"].inc(V.scalar_tensor_tensor(
                    p_sb[:, 133 * buf:133 * (buf + 1)],
                    p_sb[:, 133 * buf:133 * (buf + 1)], 1.0,
                    mf_sb[:, 133 * buf:133 * (buf + 1)],
                    ALU.mult, ALU.mult,
                    accum_out=stats[:, 64 + buf:65 + buf]))
                ev_pg[(j, h)] = e_pg
                S["dve"].wait(V, e_pg)
                dcol = dead_sb[:, NH * j + h:NH * j + h + 1]
                e = S["dve"].inc(V.tensor_scalar(dcol, stats[:, 64 + buf:65 + buf],
                                                 0.0, None, ALU.is_le))
                if j == 0:
                    S["dve"].wait(V, e)
                    S["dve"].inc(V.tensor_scalar(dcol, dcol, q0dm_sb[:, 0:1],
                                                 None, ALU.mult))
                e = S["dve"].inc(V.tensor_scalar_max(stats[:, 70 + buf:71 + buf],
                                                     stats[:, 64 + buf:65 + buf],
                                                     1e-30))
                S["dve"].wait(V, e)
                e = S["dve"].inc(V.reciprocal(stats[:, 74 + buf:75 + buf],
                                              stats[:, 70 + buf:71 + buf]))
                S["dve"].wait(V, e)
                ev_pn[(j, h)] = S["dve"].inc(V.tensor_scalar_mul(
                    pn_sb[:, 133 * buf:133 * (buf + 1)],
                    p_sb[:, 133 * buf:133 * (buf + 1)],
                    stats[:, 74 + buf:75 + buf]))
                ev_pnwar[(j, h)] = 0

            def emit_ctx(j, h):
                buf = h % 4
                pnT_sb = pnT_sb0 if h % 2 == 0 else pnT_alt
                pT_prev = (j, h - 2) if h >= 2 else (j - 1, h + 14)
                if pT_prev in ev_pnTev:
                    S["act"].wait(T, ev_pnTev[pT_prev])
                S["dve"].wait(T, ev_pn[(j, h)])
                if j == 0 and h == 0:
                    S["dve"].wait(T, e_vones)
                T.transpose(pb6b[:, 0:128], pn_sb[:, 133 * buf:133 * buf + 128],
                            identb[:, :])
                T.transpose(pb6b[0:4, 128:256],
                            pn_sb[:, 133 * buf + 128:133 * buf + 132],
                            identb[:, :])
                e_tr = S["pe"].inc(T.transpose(
                    pb6b[0:1, 256:384],
                    pn_sb[:, 133 * buf + 132:133 * buf + 133], identb[:, :]))
                ev_pnwar[(j, h)] = e_tr
                cprev2 = (j, h - 2) if h >= 2 else (j - 1, h + 14)
                if cprev2 in ev_ctx:
                    S["pe"].wait(C, ev_ctx[cprev2])
                S["pe"].wait(C, e_tr)
                C.copy(pnT_sb[:, 0:128], pb6b[:, 0:128])
                C.copy(pnT_sb[0:4, 128:256], pb6b[0:4, 128:256])
                ev_pnTev[(j, h)] = S["act"].inc(
                    C.copy(pnT_sb[0:1, 256:384], pb6b[0:1, 256:384]))
                cb = pb[2 + (h % 2)]
                cprev = (j, h - 2) if h >= 2 else (j - 1, h + 14)
                if cprev in ev_ctxev:
                    S["act"].wait(T, ev_ctxev[cprev])
                S["act"].wait(T, ev_pnTev[(j, h)])
                T.matmul(cb[0:64, 0:128], vtile(j)[:, 64 * h:64 * h + 64],
                         pnT_sb[:, 0:128], start=True, stop=False)
                T.matmul(cb[0:64, 0:128], vtile(j + 1)[0:4, 64 * h:64 * h + 64],
                         pnT_sb[0:4, 128:256], start=False, stop=False)
                ev_ctx[(j, h)] = S["pe"].inc(T.matmul(
                    cb[0:64, 0:128], v0row[0:1, 64 * h:64 * h + 64],
                    pnT_sb[0:1, 256:384], start=False, stop=True))
                S["pe"].wait(C, ev_ctx[(j, h)])
                ev_ctxev[(j, h)] = S["act"].inc(C.copy(
                    ctxfm[64 * (h % 2):64 * (h % 2) + 64,
                          128 * (h // 2):128 * (h // 2) + 128],
                    cb[0:64, 0:128]))

            S["act"].wait(V, e_q_act)
            e_o0ms = S["dve"].inc(V.memset(o0_sb[0:17, :], 0.0))
            e_ones0b = S["dve"].inc(V.memset(ones0b[0:1, :], 1.0))
            S["act"].wait(G, e_q_act)
            e_bo = S["gfet"].inc(G.dma_start(out=brow2[0:1, :], in_=b_d["bo"]), 16)
            e_s0exp_prev = 0
            e_o0g2_prev = 0
            e_o0g13_prev = e_o0ms
            e_oproj_prev = 0
            e_band_last_dve = 0
            ev_biaslast = 0
            for j in range(NBAND):
                bbv = 0 if j == 0 else (2 if j == 15 else 1)
                S["dve"].wait(V, ev_biaslast)
                V.tensor_copy(bias133[:, 0:132],
                              bb_sb[:, 132 * bbv:132 * (bbv + 1)])
                e_b133 = S["dve"].inc(V.tensor_copy(bias133[:, 132:133],
                                                    k0b_sb[:, j:j + 1]))
                S["dve"].wait(V, e_b133)
                for hh0 in range(4):
                    emit_scores(j, hh0)
                emit_epilogue_A(j, 0)
                emit_epilogue_A(j, 1)
                for h in range(NH):
                    if h + 2 < NH:
                        emit_epilogue_A(j, h + 2)
                    emit_epilogue_B(j, h)
                    if h + 4 < NH:
                        emit_scores(j, h + 4)
                    emit_ctx(j, h)
                    if h == NH - 1:
                        ev_biaslast = ev_gate[(j, h)]
                if not BAND_OPROJ:
                    continue
                S["awo"].wait(T, ev_wo)
                S["gfet"].wait(T, e_bo)
                S["dve"].wait(T, e_ones0b)
                for dh in range(2):
                    S["dve"].wait(T, e_oproj_prev)
                    S["act"].wait(T, ev_ctxev[(j, 15)])
                    for ic in range(8):
                        T.matmul(pb[7][:, :], ctxfm[:, 128 * ic:128 * (ic + 1)],
                                 wslot2(ic)[:, 512 * dh:512 * (dh + 1)],
                                 start=(ic == 0), stop=False)
                    e_mm = S["pe"].inc(T.matmul(
                        pb[7][:, :], ones0b[0:1, 0:128],
                        brow2[0:1, 512 * dh:512 * (dh + 1)],
                        start=False, stop=True))
                    S["pe"].wait(V, e_mm)
                    e_oproj_prev = S["dve"].inc(V.tensor_add(
                        xtile(j)[:, 512 * dh:512 * (dh + 1)],
                        xtile(j)[:, 512 * dh:512 * (dh + 1)], pb[7][:, :]))
                if not BAND_S0:
                    continue
                if S0LVL >= 2:
                    S["act"].wait(T, e_s0exp_prev)
                else:
                    S["dve"].wait(T, e_s0exp_prev)
                S["dve"].wait(T, e_o0g2_prev)
                if j == 0:
                    S["dve"].wait(T, e_q0m)
                S["dve"].wait(T, e_oproj_prev)
                for ic in range(8):
                    last = T.matmul(pb[7][:, 2 * ic:2 * ic + 2],
                                    kfm(ic)[:, 128 * j:128 * (j + 1)],
                                    q0m[:, 2 * ic:2 * ic + 2],
                                    start=(ic == 0), stop=(ic == 7),
                                    skip_group_check=True)
                e_s0 = S["pe"].inc(last)
                s0v = pb[7][:, 0:16]
                S["pe"].wait(V, e_s0)
                if S0LVL == 0:
                    e_mf0 = S["dve"].inc(V.tensor_copy(p0t[:, 0:16], s0v))
                    e_s0exp_prev = e_mf0
                    e_vmc = e_mf0
                    continue
                e = S["dve"].inc(V.tensor_add(
                    s0v, s0v,
                    r0b_sb[:, j:j + 1].broadcast_to([128, 16])))
                S["dve"].wait(V, e)
                e_mf0 = S["dve"].inc(V.tensor_scalar(p0t[:, 0:16], s0v,
                                                     0.0, None, ALU.is_ge))
                e_s0exp_prev = e_mf0
                e_vmc = e_mf0
                if S0LVL >= 2:
                    S["dve"].wait(C, e_mf0)
                    e_s0exp_prev = S["act"].inc(C.activation(
                        p0t[:, 16:32], s0v, AF.Exp, bias=negmf[:, 0:1], scale=1.0))
                if S0LVL >= 3:
                    S["act"].wait(V, e_s0exp_prev)
                    e_p0g = S["dve"].inc(V.scalar_tensor_tensor(
                        p0e[:, 1:17], p0t[:, 16:32], 1.0, p0t[:, 0:16],
                        ALU.mult, ALU.mult))
                    e_vmc = S["dve"].inc(V.tensor_copy(p0e[:, 0:1], vm_sb[:, j:j + 1]))
                if not BAND_O0:
                    continue
                S["dve"].wait(T, e_vmc)
                S["dve"].wait(T, e_o0g13_prev)
                S["act"].wait(T, ev_ctxev[(j, 14)])
                e1 = S["pe"].inc(T.matmul(pb[2][0:17, :], p0e[:, :],
                                          vtile(j)[:, 0:512],
                                          start=True, stop=True))
                S["pe"].wait(V, e1)
                e_add1 = S["dve"].inc(V.tensor_add(o0_sb[0:17, 0:512],
                                                   o0_sb[0:17, 0:512],
                                                   pb[2][0:17, :]))
                S["act"].wait(T, ev_ctxev[(j, 15)])
                e2 = S["pe"].inc(T.matmul(pb[3][0:17, :], p0e[:, :],
                                          vtile(j)[:, 512:1024],
                                          start=True, stop=True))
                S["pe"].wait(V, e2)
                e_o0g2_prev = S["dve"].inc(V.tensor_add(o0_sb[0:17, 512:1024],
                                                        o0_sb[0:17, 512:1024],
                                                        pb[3][0:17, :]))
                S["dve"].wait(T, e_add1)
                e3 = S["pe"].inc(T.matmul(pb[2][0:17, 0:1], p0e[:, :],
                                          vtile(j)[:, 1024:1025],
                                          start=True, stop=True))
                S["pe"].wait(V, e3)
                e_o0g13_prev = S["dve"].inc(V.tensor_add(o0_sb[0:17, 1024:1025],
                                                         o0_sb[0:17, 1024:1025],
                                                         pb[2][0:17, 0:1]))
            e_band_dve = S["dve"].n
            e_band_pe = S["pe"].n
            e_final_dve = e_band_dve

        if STAGE >= 3:
            S["dve"].wait(Y, e_band_dve)
            e_cc2i = S["st"].inc(Y.dma_start(out=cc2i[:, :], in_=o0_sb[:, :]), 16)
            S["st"].wait(G, e_cc2i)
            cci2 = G.collective_compute(
                "AllReduce", ALU.add,
                replica_groups=[[0, 1, 2, 3], [4, 5, 6, 7]],
                ins=[cc2i.ap().opt()], outs=[cc2o.ap().opt()])
            S["cc2d"].inc(cci2, 1)
            G.wait_ge(S["cc2d"].h, 1)
            e_cc2o = S["gfet"].inc(G.dma_start(out=o0_sb[:, :], in_=cc2o[:, :]), 16)

            S["gfet"].wait(V, e_cc2o)
            e = S["dve"].inc(V.tensor_scalar_max(l17[0:17, 0:1],
                                                 o0_sb[0:17, 1024:1025], 1e-30))
            S["dve"].wait(V, e)
            e = S["dve"].inc(V.reciprocal(l17[0:17, 1:2], l17[0:17, 0:1]))
            e_d17 = S["dve"].inc(V.tensor_scalar(l17[0:17, 2:3],
                                                 o0_sb[0:17, 1024:1025],
                                                 0.0, None, ALU.is_le))
            S["dve"].wait(V, e_d17)
            e_o0n = S["dve"].inc(V.tensor_scalar_mul(o0_sb[0:17, :],
                                                     o0_sb[0:17, :],
                                                     l17[0:17, 1:2]))
            S["dve"].wait(V, e_o0n)
            S["pe"].wait(V, e_band_pe)
            e_mv = S["dve"].inc(V.tensor_copy(ctxfm[0:1, 0:1024], o0_sb[0:1, 0:1024]))
            S["dve"].wait(T, e_d17)
            S["dve"].wait(T, e_oproj_prev)
            e_tr = S["pe"].inc(T.transpose(pb[7][0:1, 0:17], l17[0:17, 2:3],
                                           identf[0:17, 0:17]))
            S["pe"].wait(C, e_tr)
            e_dr = S["act"].inc(C.copy(deadr[0:1, 0:17], pb[7][0:1, 0:17]))
            e_prev_add = e_mv
            for half in range(2):
                S["act"].wait(T, e_dr)
                S["dve"].wait(T, e_prev_add)
                e_mm = S["pe"].inc(T.matmul(
                    pb[7][0:17, :], deadr[0:1, 0:17],
                    ctxfm[0:1, 512 * half:512 * (half + 1)],
                    start=True, stop=True))
                S["pe"].wait(V, e_mm)
                e_prev_add = S["dve"].inc(V.tensor_add(
                    o0_sb[0:17, 512 * half:512 * (half + 1)],
                    o0_sb[0:17, 512 * half:512 * (half + 1)], pb[7][0:17, :]))
            e_o0f = e_prev_add

            e_wovm_mm = 0
            e_pb4_ev = 0
            e_cx = 0
            e_mvs = 0
            for ic in range(8):
                S["dve"].wait(T, e_o0f)
                S["act"].wait(T, e_pb4_ev)
                S["dve"].wait(T, e_mvs)
                e_tr = S["pe"].inc(T.transpose(
                    pb[4][:, 0:17], o0_sb[0:17, 128 * ic:128 * (ic + 1)],
                    identf[0:17, 0:17]))
                S["pe"].wait(C, e_tr)
                C.copy(ctx0[0:64, ic:ic + 1], pb[4][0:64, 1 + 2 * ic:2 + 2 * ic])
                e_cx = S["act"].inc(C.copy(ctx0[64:128, ic:ic + 1],
                                           pb[4][64:128, 2 + 2 * ic:3 + 2 * ic]))
                e_pb4_ev = e_cx
                S["pe"].wait(V, e_tr)
                S["pe"].wait(V, e_wovm_mm)
                e_ms = S["dve"].inc(V.memset(mvsel[:, :], 0.0))
                S["dve"].wait(V, e_ms)
                V.tensor_copy(mvsel[0:64, 2 * ic:2 * ic + 1], pb[4][0:64, 0:1])
                e_mvs = S["dve"].inc(V.tensor_copy(
                    mvsel[64:128, 2 * ic + 1:2 * ic + 2], pb[4][64:128, 0:1]))
                S["dve"].wait(T, e_mvs)
                T.matmul(pb[2][0:16, :], mvsel[:, :], wslot2(ic)[:, 0:512],
                         start=(ic == 0), stop=(ic == 7))
                e_wovm_mm = S["pe"].inc(T.matmul(
                    pb[3][0:16, :], mvsel[:, :], wslot2(ic)[:, 512:1024],
                    start=(ic == 0), stop=(ic == 7)))
            e_prev = e_o0f
            for dh in range(2):
                S["act"].wait(T, e_cx)
                S["dve"].wait(T, e_prev)
                for ic in range(8):
                    ins = T.matmul(pb[7][0:1, :], ctx0[:, ic:ic + 1],
                                   wslot2(ic)[:, 512 * dh:512 * (dh + 1)],
                                   start=(ic == 0), stop=(ic == 7))
                e_mm = S["pe"].inc(ins)
                S["pe"].wait(V, e_mm)
                e_prev = S["dve"].inc(V.scalar_tensor_tensor(
                    x_sb[0:1, 512 * dh:512 * (dh + 1)], pb[7][0:1, :],
                    ownm_sb[0:1, 0:1], x_sb[0:1, 512 * dh:512 * (dh + 1)],
                    ALU.mult, ALU.add))
            e_rowfix = e_prev

            S["pe"].wait(C, S["pe"].n)
            C.copy(wovm[:, 0:512], pb[2][0:16, :])
            e_wovm = S["act"].inc(C.copy(wovm[:, 512:1024], pb[3][0:16, :]))

            e_dt = 0
            e_fixadd = e_rowfix
            for t in range(NT):
                S["act"].wait(T, e_dt)
                S["dve"].wait(T, e_band_dve)
                e_tr = S["pe"].inc(T.transpose(
                    pb6b[0:16, 0:128], dead_sb[:, NH * t:NH * (t + 1)],
                    identb[:, :]))
                S["pe"].wait(C, e_tr)
                e_dt = S["act"].inc(C.copy(deadt[0:16, :], pb6b[0:16, 0:128]))
                for dh in range(2):
                    S["act"].wait(T, e_dt)
                    S["act"].wait(T, e_wovm)
                    S["dve"].wait(T, e_fixadd)
                    e_mm = S["pe"].inc(T.matmul(
                        pb[5][:, :], deadt[0:16, :],
                        wovm[:, 512 * dh:512 * (dh + 1)],
                        start=True, stop=True))
                    S["pe"].wait(V, e_mm)
                    e_fixadd = S["dve"].inc(V.tensor_add(
                        xtile(t)[:, 512 * dh:512 * (dh + 1)],
                        xtile(t)[:, 512 * dh:512 * (dh + 1)], pb[5][:, :]))
            e_combine_dve = e_fixadd
            e_combine_pe = S["pe"].n

            e_ffn2, _ = emit_ffn(False, e_combine_dve, e_band_pe, e_combine_pe)
            e_final_dve = e_ffn2

        S["dve"].wait(Y, e_final_dve)
        xo = xout.rearrange("(t p) d -> p t d", p=128)
        for i in range(4):
            S["st"].inc(Y.dma_start(out=xo[:, 4 * i:4 * i + 4, :],
                                    in_=xv[:, 4 * i:4 * i + 4, :]), 16)
        Y.wait_ge(S["st"].h, S["st"].n)

        for nm, px in (("tensor", T), ("vector", V), ("scalar", C),
                       ("gpsimd", G), ("sync", Y)):
            getattr(block, nm)(px.flush)

    return nc


_CACHE = {}


def kernel(L0_attn_x, L0_attn_wq, L0_attn_bq, L0_attn_wk, L0_attn_bk,
           L0_attn_wv, L0_attn_bv, L0_attn_wo, L0_attn_bo,
           L0_ffn_router_kernel_DE, L0_ffn_kernel_gating_EDF,
           L0_ffn_kernel_up_EDF, L0_ffn_kernel_down_EFD):
    x = np.ascontiguousarray(np.asarray(L0_attn_x, dtype=np.float32))
    shared = dict(
        wqT=np.ascontiguousarray(np.asarray(L0_attn_wq, np.float32).T),
        wkT=np.ascontiguousarray(np.asarray(L0_attn_wk, np.float32).T),
        wvT=np.ascontiguousarray(np.asarray(L0_attn_wv, np.float32).T),
        woT=np.ascontiguousarray(np.asarray(L0_attn_wo, np.float32).T),
        bq=np.asarray(L0_attn_bq, np.float32).reshape(1, DM),
        bk=np.asarray(L0_attn_bk, np.float32).reshape(1, DM),
        bv=np.asarray(L0_attn_bv, np.float32).reshape(1, DM),
        bo=np.asarray(L0_attn_bo, np.float32).reshape(1, DM),
        router=np.ascontiguousarray(np.asarray(L0_ffn_router_kernel_DE, np.float32)),
        wg=np.ascontiguousarray(np.asarray(L0_ffn_kernel_gating_EDF, np.float32)),
        wu=np.ascontiguousarray(np.asarray(L0_ffn_kernel_up_EDF, np.float32)),
        wd=np.ascontiguousarray(np.asarray(L0_ffn_kernel_down_EFD, np.float32)),
        identb=np.eye(128, dtype=ml_dtypes.bfloat16),
        identf=np.eye(128, dtype=np.float32),
    )
    if "nc" not in _CACHE:
        _CACHE["nc"] = build_nc()
    nc = _CACHE["nc"]
    in_maps = []
    for core in range(8):
        b, rank = divmod(core, 4)
        m = dict(shared)
        m["xin"] = np.ascontiguousarray(x[b, rank * TPC:(rank + 1) * TPC, :])
        m.update(_host_consts(rank))
        in_maps.append(m)
    res = run_bass_kernel_spmd(nc, in_maps, core_ids=list(range(8)))
    out = np.empty((2, 8192, DM), dtype=np.float32)
    for core in range(8):
        b, rank = divmod(core, 4)
        out[b, rank * TPC:(rank + 1) * TPC, :] = res.results[core]["xout"]
    return out


if __name__ == "__main__":
    nc = build_nc()
    print("build ok")
